# version 07
# baseline (speedup 1.0000x reference)
import contextlib

import numpy as np
import ml_dtypes

import concourse.bass as bass
import concourse.mybir as mybir
from concourse.bass_utils import run_bass_kernel_spmd

F32 = mybir.dt.float32
BF16 = mybir.dt.bfloat16
AF = mybir.ActivationFunctionType
ALU = mybir.AluOpType
AX = mybir.AxisListType

B, S, D = 8, 4096, 2048
NCHUNK_FULL = 4
CT = 1024
NTT = CT // 128
NH, NKV, DK = 16, 4, 128
INTER, RINTER = 8192, 1024
LN_EPS = 1e-5
RMS_EPS = 1e-8
ROPE_BASE = 10000.0
MASK_FILL = -10000.0
NDC = D // 128

_LAST_PLAN = None
ENGS = ("sync", "tensor", "vector", "scalar", "gpsimd")
ESEM = {"tensor": "s_pe", "vector": "s_dve", "scalar": "s_act",
        "gpsimd": "s_pool", "sync": "s_sp"}
DSEM = {"sync": "d_sp", "scalar": "d_act", "gpsimd": "d_pool"}


class Plan:

    def __init__(self, nc, enter):
        self.nc = nc
        self._enter = enter
        self.q = {e: [] for e in ENGS}
        self.cnt = {}
        self.semh = {}
        self.clock = {e: {} for e in ENGS}
        self.snap = {}
        self.res = {}
        self.selfw = {}
        self.log = []
        for name in set(ESEM.values()) | set(DSEM.values()):
            self.semh[name] = enter(nc.semaphore(name))
            self.cnt[name] = 0

    def _require(self, need, s, c):
        if c > need.get(s, 0):
            need[s] = c

    def op(self, eng, fn, reads=(), writes=(), dma=False, after=None,
           lazy=False):
        if not lazy:
            lazy = any(w.startswith("cl/") or w == "scr/rms" for w in writes)
        if dma:
            assert isinstance(dma, str), "dma= must name a stream class"
            sem = "d_" + dma
            if sem not in self.semh:
                self.semh[sem] = self._enter(self.nc.semaphore(sem))
                self.cnt[sem] = 0
        else:
            sem = ESEM[eng]
        inc = 16 if dma else 1
        need = {}
        force = {}
        for r in reads:
            rr = self.res.get(r)
            if rr and rr["w"]:
                self._require(need, *rr["w"])
                if rr.get("lazy"):
                    self._require(force, *rr["w"])
        for w in writes:
            rr = self.res.get(w)
            if rr:
                if rr["w"]:
                    self._require(need, *rr["w"])
                for s, c in rr["r"]:
                    self._require(need, s, c)
        if after:
            for s, c in after.items():
                self._require(need, s, c)
        clk = self.clock[eng]
        own = ESEM[eng]
        wl = []
        for s, c in need.items():
            if s == own:
                if (not dma) and force.get(s, 0) >= c and self.selfw.get(eng, 0) < c:
                    wl.append((s, c))
                    self.selfw[eng] = c
                continue
            if clk.get(s, 0) >= c:
                continue
            wl.append((s, c))
            snap = self.snap.get(s, {}).get(c)
            if snap:
                for s2, c2 in snap.items():
                    if clk.get(s2, 0) < c2:
                        clk[s2] = c2
            if clk.get(s, 0) < c:
                clk[s] = c
        self.cnt[sem] += inc
        cnt = self.cnt[sem]
        clk[sem] = cnt
        self.snap.setdefault(sem, {})[cnt] = dict(clk)
        semh = self.semh[sem]
        whl = [(self.semh[s], c) for s, c in wl]

        def closure(e, fn=fn, whl=whl, semh=semh, inc=inc):
            for h, c in whl:
                e.wait_ge(h, c)
            ins = fn(e)
            ins.then_inc(semh, inc)

        self.q[eng].append(closure)
        self.log.append((eng, sem, cnt, tuple(wl), tuple(reads), tuple(writes)))
        for r in reads:
            self.res.setdefault(r, {"w": None, "r": []})["r"].append((sem, cnt))
        for w in writes:
            rr = self.res.setdefault(w, {"w": None, "r": []})
            rr["w"] = (sem, cnt)
            rr["r"] = []
            rr["lazy"] = lazy
        return (sem, cnt)

    def fence(self, *prefixes):
        mark = {}
        for name, rr in list(self.res.items()):
            if any(name.startswith(p) for p in prefixes):
                if rr["w"]:
                    self._require(mark, *rr["w"])
                for s, c in rr["r"]:
                    self._require(mark, s, c)
                del self.res[name]
        return mark

    def seed(self, mark, *names):
        for n in names:
            self.res[n] = {"w": None, "r": list(mark.items())}

    def raw(self, eng, fn):
        self.q[eng].append(fn)



_PERM = np.concatenate([np.arange(0, 128, 2), np.arange(1, 128, 2)])


def _tile_lhs(wt, n_oc):
    d, o = wt.shape
    arr = wt.reshape(d // 128, 128, n_oc, o // n_oc)
    return np.ascontiguousarray(arr.transpose(2, 1, 0, 3))


def _host_prep(inp):
    f = np.float32
    wq = inp["L0_attn_Wq"].astype(f)
    wk = inp["L0_attn_Wk"].astype(f)
    wv = inp["L0_attn_Wv"].astype(f)
    wo = inp["L0_attn_Wo"].astype(f)
    wq_p = wq.reshape(NH, DK, D)[:, _PERM, :].reshape(NH * DK, D)
    wk_p = wk.reshape(NKV, DK, D)[:, _PERM, :].reshape(NKV * DK, D)
    bq_p = inp["L0_attn_bq"].astype(f).reshape(NH, DK)[:, _PERM]
    bk_p = inp["L0_attn_bk"].astype(f).reshape(NKV, DK)[:, _PERM]

    out = {}
    out["wq"] = _tile_lhs(np.ascontiguousarray(wq_p.T), NH)
    out["wk"] = _tile_lhs(np.ascontiguousarray(wk_p.T), NKV)
    out["wv"] = _tile_lhs(np.ascontiguousarray(wv.T), NKV)
    wot = np.ascontiguousarray(wo.T)
    out["wo"] = np.ascontiguousarray(
        wot.reshape(NDC, 128, 4, 512).transpose(2, 1, 0, 3))
    gwt = np.ascontiguousarray(inp["L0_ffn_sh_gate"].astype(f).T)
    uwt = np.ascontiguousarray(inp["L0_ffn_sh_up"].astype(f).T)
    out["gw"] = _tile_lhs(gwt, INTER // 128)
    out["uw"] = _tile_lhs(uwt, INTER // 128)
    dwt = np.ascontiguousarray(inp["L0_ffn_sh_down"].astype(f).T)
    out["dwm"] = np.ascontiguousarray(
        dwt.reshape(8, 8, 128, 4, 512).transpose(0, 3, 2, 1, 4))
    rgt = np.ascontiguousarray(inp["L0_ffn_r_gate"].astype(f).T)
    rut = np.ascontiguousarray(inp["L0_ffn_r_up"].astype(f).T)
    out["rg"] = _tile_lhs(rgt, RINTER // 128)
    out["ru"] = _tile_lhs(rut, RINTER // 128)
    rdt = np.ascontiguousarray(inp["L0_ffn_r_down"].astype(f).T)
    out["rd"] = np.ascontiguousarray(
        rdt.reshape(8, 128, 4, 512).transpose(2, 1, 0, 3))
    out["rw"] = np.ascontiguousarray(
        inp["L0_ffn_router_weight"].astype(f).T.reshape(NDC, 128, 8))

    out["bqc"] = np.ascontiguousarray(bq_p.T)
    out["bkc"] = np.ascontiguousarray(bk_p.T)
    out["bvc"] = np.ascontiguousarray(
        inp["L0_attn_bv"].astype(f).reshape(NKV, DK).T)
    out["bob"] = np.ascontiguousarray(
        np.broadcast_to(inp["L0_attn_bo"].astype(f)[None, :], (128, D)))
    out["qnc"] = np.ascontiguousarray(
        inp["L0_attn_qn_w"].astype(f)[_PERM].reshape(128, 1))
    out["knc"] = np.ascontiguousarray(
        inp["L0_attn_kn_w"].astype(f)[_PERM].reshape(128, 1))
    out["sclc"] = np.ascontiguousarray(
        np.broadcast_to(np.asarray(inp["L0_attn_scale"], f).reshape(1, 1), (128, 1)))

    j = np.arange(0, DK, 2, dtype=np.float64) / DK
    inv_freq = (1.0 / (ROPE_BASE ** j)).astype(f)
    t = np.arange(S, dtype=f)
    theta = (t[:, None] * inv_freq[None, :]).astype(f)
    cos_t = np.cos(theta.astype(np.float64)).astype(f).T
    sin_t = np.sin(theta.astype(np.float64)).astype(f).T
    cstab = np.stack([np.concatenate([cos_t, cos_t], axis=0),
                      np.concatenate([-sin_t, sin_t], axis=0)], axis=1)
    out["cstab"] = np.ascontiguousarray(cstab.transpose(0, 1, 2)) \
        if cstab.shape == (128, 2, 4096) else np.ascontiguousarray(cstab)
    pm = np.zeros((128, 128), f)
    pm[np.arange(128), (np.arange(128) + 64) % 128] = 1.0
    out["pshift"] = pm

    qi = np.arange(128)[:, None]
    ki = np.arange(128)[None, :]
    keep_prev = (ki >= qi + 64).astype(f)
    keep_diag = ((qi - ki >= 0) & (qi - ki <= 64)).astype(f)
    keep = np.concatenate([keep_prev, keep_diag], axis=1)
    out["mkeep"] = np.ascontiguousarray(keep)
    out["mfill"] = np.ascontiguousarray((1.0 - keep) * MASK_FILL)
    out["mk8"] = np.ascontiguousarray(keep.astype(np.uint8))
    pm = out.pop("pshift")
    cpk = np.zeros((128, 928), f)
    cpk[:, 0:256] = out.pop("mkeep")
    cpk[:, 256:512] = out.pop("mfill")
    cpk[:, 512:640] = 1.0
    cpk[:, 640:768] = pm
    cpk[:, 768:896] = np.eye(128, dtype=f)
    cpk[:, 896:912] = out.pop("bqc")
    cpk[:, 912:916] = out.pop("bkc")
    cpk[:, 916:920] = out.pop("bvc")
    cpk[:, 920:921] = out.pop("qnc")
    cpk[:, 921:922] = out.pop("knc")
    cpk[:, 922:923] = out.pop("sclc")
    cpk[:, 923:925] = np.array([LN_EPS, RMS_EPS], f)[None, :]
    out["cpk"] = np.ascontiguousarray(cpk)
    return out



def build(nchunk=NCHUNK_FULL, dbg=()):
    nc = bass.Bass("TRN2", target_bir_lowering=False, debug=False)

    dr = {}
    def din(name, shape, dtype=F32):
        dr[name] = nc.dram_tensor(name, list(shape), dtype, kind="ExternalInput").ap()
        return dr[name]

    x_h = din("x", (S, D))
    out_h = nc.dram_tensor("out", [S, D], F32, kind="ExternalOutput").ap()
    din("wq", (NH, 128, NDC, 128)); din("wk", (NKV, 128, NDC, 128))
    din("wv", (NKV, 128, NDC, 128)); din("wo", (4, 128, NDC, 512))
    din("gw", (64, 128, NDC, 128)); din("uw", (64, 128, NDC, 128))
    din("dwm", (8, 4, 128, 8, 512))
    din("rg", (8, 128, NDC, 128)); din("ru", (8, 128, NDC, 128))
    din("rd", (4, 128, 8, 512)); din("rw", (NDC, 128, 8))
    din("bob", (128, D)); din("cstab", (128, 2, S))
    din("cpk", (128, 928)); din("mk8", (128, 256), mybir.dt.uint8)

    dbg_h = {}
    for name, (shape, dt) in dict(
        lnfm=((128, NDC, CT), BF16), qfm=((128, NH, CT), BF16),
        kfm=((128, NKV, 128 + CT), BF16), vtm=((128, 1 + NTT, 512), BF16),
        attno=((128, NH, CT), BF16), xmid=((CT, D), F32),
        comb=((128, NTT), F32), hbuf=((128, 8, CT), BF16),
        oacc=((128, NTT, D), F32), adbg=((128, 2, 256 * 4), F32),
        adbg2=((128, 2, 512), BF16), mdbg=((128, 512), F32),
    ).items():
        if name in dbg:
            dbg_h[name] = nc.dram_tensor(
                "dbg_" + name, list(shape), dt, kind="ExternalOutput").ap()

    with contextlib.ExitStack() as stack:
        E = stack.enter_context
        A = nc.alloc_sbuf_tensor

        xa = A("xa", [128, NTT, D], F32)
        lnfm = A("lnfm", [128, NDC, CT], BF16)
        qa = A("qa", [128, NH, CT], BF16)
        rkv = A("rkv", [128, 24576 // 2], BF16)
        kvaddr = nc.lookup_mloc(rkv).addr
        AT = nc.alloc_sbuf_tensor_at
        k_fm = AT("k_fm", [128, NKV, 128 + CT], BF16, offset=kvaddr)
        v_tm = AT("v_tm", [128, 1 + NTT, 512], BF16, offset=kvaddr + 9216)
        h_buf = AT("h_buf", [128, 8, CT], BF16, offset=kvaddr)
        wgu = AT("wgu", [128, 2, NDC, 128], BF16, offset=kvaddr + 16384)
        wodw = A("wodw", [128, 2, 8, 512], BF16)
        wA = A("wA", [128, 2, 8, 128], BF16)

        scr = A("scr", [128, 20480 // 4], F32)
        sa = nc.lookup_mloc(scr).addr
        s_sq = AT("s_sq", [128, 2, 512], F32, offset=sa)
        s_qraw = AT("s_qraw", [128, 2, 512], F32, offset=sa + 4096)
        s_qn = AT("s_qn", [128, 512], F32, offset=sa + 8192)
        s_rstd = AT("s_rstd", [128, 512], F32, offset=sa + 10240)
        s_rope = AT("s_rope", [128, 2, 512], F32, offset=sa + 12288)
        s_rms = AT("s_rms", [1, 512], F32, offset=sa + 16384)
        s_vfm = AT("s_vfm", [128, CT], BF16, offset=sa + 18432)
        s_lntm = AT("s_lntm", [128, D], BF16, offset=sa)
        s_big1 = AT("s_big1", [128, D], F32, offset=sa + 4096)
        s_big2 = AT("s_big2", [128, D], F32, offset=sa + 12288)
        s_sc = AT("s_sc", [128, 4, 256], F32, offset=sa)
        s_e = AT("s_e", [128, 4, 256], F32, offset=sa + 4096)
        s_ebf = AT("s_ebf", [128, 6, 256], BF16, offset=sa + 8192)
        s_aT = AT("s_aT", [128, 2, 256], BF16, offset=sa + 11264)
        s_sig = AT("s_sig", [128, 2, 512], F32, offset=sa)
        s_lnsq = s_big1
        s_bo = s_big2
        s_l1t = s_big1
        s_xstg = s_big2
        s_tmp8 = s_big1

        cst2 = A("cst2", [128, 2, CT], F32)
        ct = cst2[:, 0, :]
        st = cst2[:, 1, :]
        cpk = A("cpk_s", [128, 928], F32)
        mkeep = cpk[:, 0:256]
        mfill = cpk[:, 256:512]
        ones = cpk[:, 512:640]
        psh = cpk[:, 640:768]
        identf = cpk[:, 768:896]
        bqc = cpk[:, 896:912]
        bkc = cpk[:, 912:916]
        bvc = cpk[:, 916:920]
        qnc = cpk[:, 920:921]
        knc = cpk[:, 921:922]
        sclc = cpk[:, 922:923]
        cstc = cpk[:, 923:925]
        ident_t = A("ident_s", [128, 128], BF16)
        ident = ident_t[:, :]
        mk8 = A("mk8_s", [128, 256], mybir.dt.uint8)
        qscl = A("qscl_s", [128, 1], F32)
        rw_sb = A("rw_sb", [128, NDC, 8], BF16)
        comb = A("comb_s", [128, NTT], F32)
        cols = A("cols_s", [128, 80], F32)
        k_halo = A("k_halo", [128, NKV, 128], BF16)
        v_halo = A("v_halo", [128, 512], BF16)

        oproj = AT("oproj", [128, 4, D], F32, offset=nc.lookup_mloc(lnfm).addr)

        pb = [E(nc.psum_tensor(f"pb{i}", [128, 512], F32)) for i in range(6)]
        pbT = [E(nc.psum_tensor(f"pbT{i}", [128, 1024], BF16)) for i in range(2)]

        p = Plan(nc, E)

        def col(par, i):
            return cols[:, par * 32 + i:par * 32 + i + 1]

        p.op("sync", lambda e: e.dma_start(out=cpk[:, :], in_=dr["cpk"]),
             writes=("c/all",), dma="cpk")
        p.op("sync", lambda e: e.dma_start(out=mk8[:], in_=dr["mk8"]),
             writes=("c/mk8",), dma="mk8")
        p.op("vector", lambda e: e.tensor_mul(qscl[:], qnc, sclc),
             reads=("c/all",), writes=("c/qscl",))
        p.op("vector", lambda e: e.tensor_copy(ident_t[:], identf),
             reads=("c/all",), writes=("c/ident16",))
        p.op("gpsimd", lambda e: e.dma_start(
            out=rw_sb[:], in_=dr["rw"].rearrange("dc p o -> p dc o")),
            writes=("c/rw",), dma="rw")


        def phase_ln(c, dump=None):
            fk = p.fence("scr/")
            p.seed(fk, "scr/big1", "scr/lntm")
            def cl(base, tt):
                return cols[:, base + tt:base + tt + 1]
            for grp in range(2):
                tts = range(grp * 4, grp * 4 + 4)
                for tt in tts:
                    p.op("vector", lambda e, tt=tt: e.reduce_sum(
                         cl(0, tt), xa[:, tt, :], axis=AX.X),
                         reads=(f"xa/{tt}",), writes=(f"cl/s{tt}",), lazy=True)
                for tt in tts:
                    p.op("scalar", lambda e, tt=tt: e.activation(
                         s_lnsq[:], xa[:, tt, :], AF.Square, accum_out=cl(8, tt)),
                         reads=(f"xa/{tt}",), writes=("scr/big1", f"cl/q{tt}"),
                         lazy=True)
                for tt in tts:
                    p.op("vector", lambda e, tt=tt: e.tensor_scalar_mul(
                         cl(16, tt), cl(0, tt), 1.0 / D),
                         reads=(f"cl/s{tt}",), writes=(f"cl/m{tt}",))
                for tt in tts:
                    p.op("scalar", lambda e, tt=tt: e.activation(
                         cl(24, tt), cl(16, tt), AF.Square),
                         reads=(f"cl/m{tt}",), writes=(f"cl/m2{tt}",))
                for tt in tts:
                    p.op("vector", lambda e, tt=tt: e.scalar_tensor_tensor(
                         cl(32, tt), cl(8, tt), 1.0 / D, cl(24, tt),
                         ALU.mult, ALU.subtract),
                         reads=(f"cl/q{tt}", f"cl/m2{tt}"), writes=(f"cl/v{tt}",))
                for tt in tts:
                    p.op("scalar", lambda e, tt=tt: e.activation(
                         cl(40, tt), cl(32, tt), AF.Sqrt, bias=cstc[:, 0:1]),
                         reads=(f"cl/v{tt}", "c/all"), writes=(f"cl/sr{tt}",))
                for tt in tts:
                    p.op("vector", lambda e, tt=tt: e.reciprocal(
                         cl(48, tt), cl(40, tt)),
                         reads=(f"cl/sr{tt}",), writes=(f"cl/r{tt}",), lazy=True)
                for tt in tts:
                    p.op("vector", lambda e, tt=tt: e.tensor_scalar(
                         s_lntm[:], xa[:, tt, :], cl(16, tt), cl(48, tt),
                         ALU.subtract, ALU.mult),
                         reads=(f"xa/{tt}", f"cl/m{tt}", f"cl/r{tt}"),
                         writes=("scr/lntm",))
                    for half in range(2):
                        tb = (tt * 2 + half) % 2
                        for j in range(8):
                            dc = half * 8 + j
                            p.op("tensor", lambda e, dc=dc, j=j, tb=tb: e.transpose(
                                 pbT[tb][:, j * 128:(j + 1) * 128],
                                 s_lntm[:, dc * 128:(dc + 1) * 128], ident),
                                 reads=("scr/lntm", "c/ident16"), writes=(f"psT/{tb}",))
                        p.op("scalar", lambda e, tt=tt, half=half, tb=tb: e.copy(
                             lnfm[:, half * 8:(half + 1) * 8, tt * 128:(tt + 1) * 128],
                             pbT[tb].rearrange("p (a b) -> p a b", a=8)),
                             reads=(f"psT/{tb}",),
                             writes=tuple(f"lnfm/{half*8+j}/{tt}" for j in range(8)))
            if dump is not None:
                p.op("sync", lambda e: e.dma_start(out=dump, in_=lnfm[:]),
                     reads=tuple(f"lnfm/{dc}/{tt}" for dc in range(NDC)
                                 for tt in range(NTT)), dma="dbg")

        def stream_slab(par, src_ap, kind="wA"):
            buf = {"wA": wA, "wgu": wgu, "wodw": wodw}[kind]
            return p.op("gpsimd",
                        lambda e, b=buf, par=par, s=src_ap:
                        e.dma_start(out=b[:, par, :, :], in_=s),
                        writes=(f"{kind}/{par}",), dma=f"{kind}{par}")

        def qkv_head(c, oc, tg, wpars, kind, u):
            bank = u % 3
            par = u % 2
            lhs_reads = tuple(f"wA/{wp}" for wp in wpars)
            rhs_reads = tuple(f"lnfm/{dc}/{tt}" for dc in range(NDC)
                              for tt in range(tg * 4, tg * 4 + 4))
            for dc in range(NDC):
                wp = wpars[dc // 8]
                p.op("tensor", lambda e, dc=dc, wp=wp, bank=bank, tg=tg:
                     e.matmul(pb[bank][:, :], wA[:, wp, dc % 8, :],
                              lnfm[:, dc, tg * 512:(tg + 1) * 512],
                              start=(dc == 0), stop=(dc == NDC - 1)),
                     reads=lhs_reads + rhs_reads, writes=(f"ps/{bank}",))
            bias = {"q": bqc, "k": bkc}[kind][:, oc:oc + 1]
            p.op("scalar", lambda e, bank=bank, bias=bias, par=par:
                 e.activation(s_sq[:, par, :], pb[bank][:], AF.Square, bias=bias),
                 reads=(f"ps/{bank}", "c/all"), writes=(f"scr/sq{par}",))
            p.op("scalar", lambda e, bank=bank, bias=bias, par=par:
                 e.activation(s_qraw[:, par, :], pb[bank][:], AF.Identity,
                              bias=bias),
                 reads=(f"ps/{bank}",), writes=(f"scr/qraw{par}",))

        def qkv_tail(c, oc, tg, kind, u):
            par = u % 2
            rot_bank = 3 if u % 2 == 0 else 5
            nw = qscl[:, 0:1] if kind == "q" else knc
            p.op("tensor", lambda e, par=par: e.matmul(
                 pb[4][0:1, :], ones[:, 0:1], s_sq[:, par, :],
                 start=True, stop=True),
                 reads=(f"scr/sq{par}", "c/all"), writes=("ps/4",))
            p.op("scalar", lambda e: e.activation(
                 s_rms[0:1, :], pb[4][0:1, :], AF.Identity, scale=1.0 / DK),
                 reads=("ps/4",), writes=("scr/rms",))
            p.op("tensor", lambda e: e.matmul(
                 pb[4][:, :], ones[0:1, 0:128], s_rms[0:1, :],
                 start=True, stop=True),
                 reads=("scr/rms",), writes=("ps/4",))
            p.op("scalar", lambda e: e.activation(
                 s_rstd[:], pb[4][:, :], AF.Sqrt, bias=cstc[:, 1:2]),
                 reads=("ps/4", "c/all"), writes=("scr/rstd",))
            p.op("vector", lambda e: e.reciprocal(s_rstd[:], s_rstd[:]),
                 reads=("scr/rstd",), writes=("scr/rstd",))
            p.op("vector", lambda e, nw=nw, par=par: e.scalar_tensor_tensor(
                 s_qn[:], s_rstd[:], nw, s_qraw[:, par, :],
                 ALU.mult, ALU.mult),
                 reads=("scr/rstd", f"scr/qraw{par}", "c/all"),
                 writes=("scr/qn",))
            p.op("tensor", lambda e, rot_bank=rot_bank: e.matmul(
                 pb[rot_bank][:, :], psh, s_qn[:], start=True, stop=True),
                 reads=("scr/qn", "c/all"), writes=(f"ps/{rot_bank}",))
            csl = ct[:, tg * 512:(tg + 1) * 512]
            snl = st[:, tg * 512:(tg + 1) * 512]
            if kind == "q":
                dst = qa[:, oc, tg * 512:(tg + 1) * 512]
                dres = tuple(f"qa/{oc}/{qt}" for qt in range(tg * 4, tg * 4 + 4))
            else:
                dst = k_fm[:, oc, 128 + tg * 512:128 + (tg + 1) * 512]
                dres = tuple(f"kfm/{oc}/{qt}" for qt in range(tg * 4, tg * 4 + 4))
            p.op("vector", lambda e, csl=csl: e.tensor_mul(
                 s_rope[:, 0, :], s_qn[:], csl),
                 reads=("scr/qn", "c/cos"), writes=("scr/rope0",))
            p.op("vector", lambda e, snl=snl, rot_bank=rot_bank: e.tensor_mul(
                 s_rope[:, 1, :], pb[rot_bank][:, :], snl),
                 reads=(f"ps/{rot_bank}", "c/cos"), writes=("scr/rope1",))
            p.op("vector", lambda e, dst=dst: e.tensor_add(
                 dst, s_rope[:, 0, :], s_rope[:, 1, :]),
                 reads=("scr/rope0", "scr/rope1"), writes=dres)

        def v_unit(c, oc, wpars):
            bank0, bank1 = 0, 1
            lhs_reads = tuple(f"wA/{wp}" for wp in wpars)
            for tg in range(2):
                bank = (bank0, bank1)[tg]
                rhs_reads = tuple(f"lnfm/{dc}/{tt}" for dc in range(NDC)
                                  for tt in range(tg * 4, tg * 4 + 4))
                for dc in range(NDC):
                    wp = wpars[dc // 8]
                    p.op("tensor", lambda e, dc=dc, wp=wp, bank=bank, tg=tg:
                         e.matmul(pb[bank][:, :], wA[:, wp, dc % 8, :],
                                  lnfm[:, dc, tg * 512:(tg + 1) * 512],
                                  start=(dc == 0), stop=(dc == NDC - 1)),
                         reads=lhs_reads + rhs_reads, writes=(f"ps/{bank}",))
                p.op("scalar", lambda e, bank=bank, tg=tg, oc=oc:
                     e.activation(s_vfm[:, tg * 512:(tg + 1) * 512], pb[bank][:],
                                  AF.Identity, bias=bvc[:, oc:oc + 1]),
                     reads=(f"ps/{bank}", "c/all"), writes=("scr/vfm",))
            for ti in range(NTT):
                tb = ti % 2
                p.op("tensor", lambda e, ti=ti, tb=tb: e.transpose(
                     pbT[tb][:, (ti // 2) * 128:(ti // 2 + 1) * 128],
                     s_vfm[:, ti * 128:(ti + 1) * 128], ident),
                     reads=("scr/vfm", "c/ident16"), writes=(f"psT/{tb}",))
            for tb in range(2):
                p.op("scalar", lambda e, oc=oc, tb=tb: e.copy(
                     v_tm[:, 1 + tb:1 + NTT:2, oc * 128:(oc + 1) * 128],
                     pbT[tb][:, 0:512].rearrange("p (a b) -> p a b", a=4)),
                     reads=(f"psT/{tb}",),
                     writes=tuple(f"vtm/{1+ti}" for ti in range(tb, NTT, 2)))

        def phase_qkv(c):
            fkv = p.fence("hbuf/", "wgu/") if c > 0 else {}
            fscr = p.fence("scr/")
            p.seed(fscr, "scr/sq0", "scr/sq1", "scr/qraw0", "scr/qraw1",
                   "scr/qn", "scr/rstd", "scr/rms", "scr/rope0", "scr/rope1",
                   "scr/vfm")
            if c > 0:
                p.op("gpsimd", lambda e: e.tensor_copy(
                     k_fm[:, :, 0:128], k_halo[:]),
                     reads=("khalo",), writes=tuple(f"kfm/{kv}/halo" for kv in range(NKV)),
                     after=fkv)
                p.op("gpsimd", lambda e: e.tensor_copy(v_tm[:, 0, :], v_halo[:]),
                     reads=("vhalo",), writes=("vtm/0",), after=fkv)
            sl = 0
            def load(src, par):
                nonlocal sl
                r = stream_slab(par, src, "wA")
                sl += 1
                return r
            units = ([("q", oc, dr["wq"]) for oc in range(NH)]
                     + [("k", oc, dr["wk"]) for oc in range(NKV)]
                     + [("v", oc, dr["wv"]) for oc in range(NKV)])
            pend = None
            u = 0
            for kind, oc, w in units:
                p0, p1 = (sl % 2), ((sl + 1) % 2)
                load(w[oc, :, 0:8, :], p0)
                load(w[oc, :, 8:16, :], p1)
                if kind == "v":
                    if pend is not None:
                        qkv_tail(c, *pend)
                        pend = None
                    v_unit(c, oc, (p0, p1))
                else:
                    for tg in range(2):
                        qkv_head(c, oc, tg, (p0, p1), kind, u)
                        if pend is not None:
                            qkv_tail(c, *pend)
                        pend = (oc, tg, kind, u)
                        u += 1
            if pend is not None:
                qkv_tail(c, *pend)
            p.op("gpsimd", lambda e: e.tensor_copy(
                 k_halo[:], k_fm[:, :, CT:CT + 128]),
                 reads=tuple(f"kfm/{kv}/{NTT-1}" for kv in range(NKV)),
                 writes=("khalo",))
            p.op("gpsimd", lambda e: e.tensor_copy(v_halo[:], v_tm[:, NTT, :]),
                 reads=(f"vtm/{NTT}",), writes=("vhalo",))

        def attn_head(c, qt, h):
            u = qt * NH + h
            par = u % 4
            par6 = u % 6
            kv = h // 4
            first = (c == 0 and qt == 0)
            N = 128 if first else 256
            moff = 256 - N
            kcol0 = qt * 128 + (128 if first else 0)
            sb = 4 + u % 2
            def acol(j, par6=par6):
                return cols[:, 56 + par6 * 3 + j:56 + par6 * 3 + j + 1]
            kres = (f"kfm/{kv}/halo",) if qt == 0 and not first else ()
            kres = kres + tuple(f"kfm/{kv}/{t}" for t in range(max(0, qt - 1), qt + 1))
            p.op("tensor", lambda e, h=h, qt=qt, kv=kv, N=N, kcol0=kcol0, sb=sb:
                 e.matmul(pb[sb][:, 0:N], qa[:, h, qt * 128:(qt + 1) * 128],
                          k_fm[:, kv, kcol0:kcol0 + N], start=True, stop=True),
                 reads=(f"qa/{h}/{qt}",) + kres, writes=(f"ps/{sb}",))
            p.op("vector", lambda e, par=par, N=N, moff=moff, sb=sb:
                 e.copy_predicated(s_sc[:, par, moff:moff + N],
                                   mk8[:, moff:moff + N], pb[sb][:, 0:N]),
                 reads=(f"ps/{sb}", "c/mk8"), writes=(f"scr/sc{par}",))
            p.op("vector", lambda e, par=par, N=N, moff=moff: e.reduce_max(
                 acol(0), s_sc[:, par, moff:moff + N], axis=AX.X, negate=True),
                 reads=(f"scr/sc{par}",), writes=(f"cl/a{par6}/0",), lazy=True)
            p.op("scalar", lambda e, par=par, N=N, moff=moff: e.activation(
                 s_e[:, par, moff:moff + N], s_sc[:, par, moff:moff + N], AF.Exp,
                 bias=acol(0), accum_out=acol(1)),
                 reads=(f"scr/sc{par}", f"cl/a{par6}/0"),
                 writes=(f"scr/e{par}", f"cl/a{par6}/1"), lazy=True)
            p.op("vector", lambda e: e.reciprocal(acol(2), acol(1)),
                 reads=(f"cl/a{par6}/1",), writes=(f"cl/a{par6}/2",), lazy=True)
            p.op("vector", lambda e, par=par, par6=par6, N=N, moff=moff:
                 e.tensor_scalar_mul(
                     s_ebf[:, par6, moff:moff + N],
                     s_e[:, par, moff:moff + N], acol(2)),
                 reads=(f"scr/e{par}", f"cl/a{par6}/2"),
                 writes=(f"scr/ebf{par6}",))

        def attn_tail(c, qt, h):
            u = qt * NH + h
            par = u % 2
            par6 = u % 6
            kv = h // 4
            first = (c == 0 and qt == 0)
            N = 128 if first else 256
            nb = N // 128
            moff = 256 - N
            for b in range(nb):
                p.op("tensor", lambda e, par=par, par6=par6, b=b, moff=moff:
                     e.transpose(
                         pbT[par][:, b * 128:(b + 1) * 128],
                         s_ebf[:, par6, moff + b * 128:moff + (b + 1) * 128],
                         ident),
                     reads=(f"scr/ebf{par6}", "c/ident16"), writes=(f"psT/{par}",))
            p.op("scalar", lambda e, par=par, N=N: e.copy(
                 s_aT[:, par, 0:N], pbT[par][:, 0:N]),
                 reads=(f"psT/{par}",), writes=(f"scr/aT{par}",))
            vres = () if first else (f"vtm/{qt}",)
            vres = vres + (f"vtm/{qt + 1}",)
            for b in range(nb):
                vt = qt + b + (1 if first else 0)
                p.op("tensor", lambda e, par=par, b=b, vt=vt, kv=kv, nb=nb:
                     e.matmul(pb[par][:, 0:128],
                              v_tm[:, vt, kv * 128:(kv + 1) * 128],
                              s_aT[:, par, b * 128:(b + 1) * 128],
                              start=(b == 0), stop=(b == nb - 1)),
                     reads=(f"scr/aT{par}",) + vres, writes=(f"ps/{par}",))
            p.op("scalar", lambda e, h=h, qt=qt, par=par: e.copy(
                 qa[:, h, qt * 128:(qt + 1) * 128], pb[par][:, 0:128]),
                 reads=(f"ps/{par}",), writes=(f"qa/{h}/{qt}",))

        def phase_attn(c):
            fk = p.fence("scr/")
            p.seed(fk, *[f"scr/sc{par}" for par in range(4)],
                   *[f"scr/e{par}" for par in range(4)],
                   *[f"scr/ebf{par}" for par in range(6)],
                   "scr/aT0", "scr/aT1")
            for par in range(4):
                p.op("vector", lambda e, par=par: e.tensor_copy(
                     s_sc[:, par, :], mfill),
                     reads=("c/all",), writes=(f"scr/sc{par}",))
            from collections import deque
            pend = deque()
            for qt in range(NTT):
                for h in range(NH):
                    attn_head(c, qt, h)
                    if len(pend) >= 4:
                        attn_tail(c, *pend.popleft())
                    pend.append((qt, h))
            while pend:
                attn_tail(c, *pend.popleft())

        def phase_wo_prefetch(c):
            for dh in range(2):
                stream_slab(dh % 2, dr["wo"][0, :, dh * 8:(dh + 1) * 8, :],
                            "wodw")

        def phase_oproj_ln1(c):
            fk = p.fence("scr/", "lnfm/")
            p.seed(fk, "scr/big1", "scr/big2",
                   *[f"oproj/{tq}" for tq in range(4)])
            p.op("sync", lambda e: e.dma_start(out=s_bo[:], in_=dr["bob"]),
                 writes=("scr/big2",), dma="bo")
            sl = 0
            for half in range(2):
                for og in range(4):
                    pars = []
                    for dh in range(2):
                        par = sl % 2
                        if not (half == 0 and og == 0):
                            stream_slab(par,
                                        dr["wo"][og, :, dh * 8:(dh + 1) * 8, :],
                                        "wodw")
                        pars.append(par)
                        sl += 1
                    for dc in range(NDC):
                        wp = pars[dc // 8]
                        for tq in range(4):
                            tt = half * 4 + tq
                            p.op("tensor",
                                 lambda e, dc=dc, wp=wp, tq=tq, tt=tt:
                                 e.matmul(pb[tq][:, :], qa[:, dc, tt * 128:(tt + 1) * 128],
                                          wodw[:, wp, dc % 8, :],
                                          start=(dc == 0), stop=(dc == NDC - 1)),
                                 reads=(f"qa/{dc}/{tt}", f"wodw/{wp}"),
                                 writes=(f"ps/{tq}",))
                    for tq in range(4):
                        tt = half * 4 + tq
                        p.op("vector", lambda e, tq=tq, og=og: e.tensor_add(
                             oproj[:, tq, og * 512:(og + 1) * 512], pb[tq][:],
                             s_bo[:, og * 512:(og + 1) * 512]),
                             reads=(f"ps/{tq}", "scr/big2"),
                             writes=(f"oproj/{tq}",))
                def cl(base, tq):
                    return cols[:, base + tq:base + tq + 1]
                for tq in range(4):
                    p.op("vector", lambda e, tq=tq: e.reduce_sum(
                         cl(0, tq), oproj[:, tq, :], axis=AX.X),
                         reads=(f"oproj/{tq}",), writes=(f"cl/s{tq}",), lazy=True)
                for tq in range(4):
                    p.op("scalar", lambda e, tq=tq: e.activation(
                         s_lnsq[:], oproj[:, tq, :], AF.Square,
                         accum_out=cl(8, tq)),
                         reads=(f"oproj/{tq}",),
                         writes=("scr/big1", f"cl/q{tq}"), lazy=True)
                for tq in range(4):
                    p.op("vector", lambda e, tq=tq: e.tensor_scalar_mul(
                         cl(16, tq), cl(0, tq), 1.0 / D),
                         reads=(f"cl/s{tq}",), writes=(f"cl/m{tq}",))
                for tq in range(4):
                    p.op("scalar", lambda e, tq=tq: e.activation(
                         cl(24, tq), cl(16, tq), AF.Square),
                         reads=(f"cl/m{tq}",), writes=(f"cl/m2{tq}",))
                for tq in range(4):
                    p.op("vector", lambda e, tq=tq: e.scalar_tensor_tensor(
                         cl(32, tq), cl(8, tq), 1.0 / D, cl(24, tq),
                         ALU.mult, ALU.subtract),
                         reads=(f"cl/q{tq}", f"cl/m2{tq}"), writes=(f"cl/v{tq}",))
                for tq in range(4):
                    p.op("scalar", lambda e, tq=tq: e.activation(
                         cl(40, tq), cl(32, tq), AF.Sqrt, bias=cstc[:, 0:1]),
                         reads=(f"cl/v{tq}", "c/all"), writes=(f"cl/sr{tq}",))
                for tq in range(4):
                    p.op("vector", lambda e, tq=tq: e.reciprocal(
                         cl(48, tq), cl(40, tq)),
                         reads=(f"cl/sr{tq}",), writes=(f"cl/r{tq}",), lazy=True)
                for tq in range(4):
                    tt = half * 4 + tq
                    p.op("vector", lambda e, tq=tq: e.tensor_scalar(
                         s_l1t[:], oproj[:, tq, :], cl(16, tq), cl(48, tq),
                         ALU.subtract, ALU.mult),
                         reads=(f"oproj/{tq}", f"cl/m{tq}", f"cl/r{tq}"),
                         writes=("scr/big1",))
                    p.op("vector", lambda e, tt=tt: e.tensor_add(
                         xa[:, tt, :], xa[:, tt, :], s_l1t[:]),
                         reads=("scr/big1", f"xa/{tt}"), writes=(f"xa/{tt}",))

        def phase_spill(c):
            p.op("sync", lambda e, c=c: e.dma_start(
                 out=out_h[c * CT:(c + 1) * CT, :].rearrange(
                     "(a p) d -> p a d", p=128),
                 in_=xa[:, :, :]),
                 reads=tuple(f"xa/{tt}" for tt in range(NTT)),
                 writes=tuple(f"out/{c}/{tt}" for tt in range(NTT)), dma="sp")

        def phase_router(c):
            fk = p.fence("scr/")
            p.seed(fk, "scr/lg", "scr/eq")
            def cl(base, tt):
                return cols[:, base + tt:base + tt + 1]
            for tt in range(NTT):
                for dc in range(NDC):
                    p.op("tensor", lambda e, dc=dc, tt=tt: e.matmul(
                         pb[4][:, 0:8], lnfm[:, dc, tt * 128:(tt + 1) * 128],
                         rw_sb[:, dc, :], start=(dc == 0), stop=(dc == NDC - 1)),
                         reads=(f"lnfm/{dc}/{tt}", "c/rw"), writes=("ps/4",))
                p.op("scalar", lambda e, tt=tt: e.copy(
                     s_sq[:, 0, tt * 8:(tt + 1) * 8], pb[4][:, 0:8]),
                     reads=("ps/4",), writes=("scr/lg",))
            for tt in range(NTT):
                p.op("vector", lambda e, tt=tt: e.reduce_max(
                     cl(0, tt), s_sq[:, 0, tt * 8:(tt + 1) * 8], axis=AX.X),
                     reads=("scr/lg",), writes=(f"cl/v0{tt}",), lazy=True)
            for tt in range(NTT):
                p.op("vector", lambda e, tt=tt: e.reduce_max(
                     cl(8, tt), s_sq[:, 0, tt * 8:(tt + 1) * 8], axis=AX.X,
                     negate=True),
                     reads=("scr/lg",), writes=(f"cl/nv{tt}",), lazy=True)
            for tt in range(NTT):
                p.op("vector", lambda e, tt=tt: e.tensor_single_scalar(
                     s_qraw[:, 0, tt * 8:(tt + 1) * 8],
                     s_sq[:, 0, tt * 8:(tt + 1) * 8], cl(0, tt), ALU.is_equal),
                     reads=("scr/lg", f"cl/v0{tt}"), writes=("scr/eq",))
            for tt in range(NTT):
                p.op("vector", lambda e, tt=tt: e.scalar_tensor_tensor(
                     s_qraw[:, 0, tt * 8:(tt + 1) * 8],
                     s_qraw[:, 0, tt * 8:(tt + 1) * 8], -1e30,
                     s_sq[:, 0, tt * 8:(tt + 1) * 8], ALU.mult, ALU.add),
                     reads=("scr/eq", "scr/lg"), writes=("scr/eq",))
            for tt in range(NTT):
                p.op("vector", lambda e, tt=tt: e.reduce_max(
                     cl(16, tt), s_qraw[:, 0, tt * 8:(tt + 1) * 8], axis=AX.X),
                     reads=("scr/eq",), writes=(f"cl/v1{tt}",), lazy=True)
            for tt in range(NTT):
                p.op("scalar", lambda e, tt=tt: e.activation(
                     cl(24, tt), cl(16, tt), AF.Exp, bias=cl(8, tt)),
                     reads=(f"cl/v1{tt}", f"cl/nv{tt}"), writes=(f"cl/e1{tt}",))
            for tt in range(NTT):
                p.op("vector", lambda e, tt=tt: e.tensor_scalar_add(
                     cl(32, tt), cl(24, tt), 1.0),
                     reads=(f"cl/e1{tt}",), writes=(f"cl/Z{tt}",))
            for tt in range(NTT):
                p.op("vector", lambda e, tt=tt: e.reciprocal(
                     cl(40, tt), cl(32, tt)),
                     reads=(f"cl/Z{tt}",), writes=(f"cl/w0{tt}",), lazy=True)
            for tt in range(NTT):
                p.op("vector", lambda e, tt=tt: e.tensor_mul(
                     cl(48, tt), cl(24, tt), cl(40, tt)),
                     reads=(f"cl/e1{tt}", f"cl/w0{tt}"), writes=(f"cl/w1{tt}",))
            for tt in range(NTT):
                p.op("vector", lambda e, tt=tt: e.tensor_add(
                     comb[:, tt:tt + 1], cl(40, tt), cl(48, tt)),
                     reads=(f"cl/w0{tt}", f"cl/w1{tt}"), writes=(f"comb/{tt}",))

        def mlp(c, nmega, wg_h, wu_h, dw_h, routed):
            if not routed:
                fk = p.fence("scr/")
                p.seed(fk, "scr/sig0", "scr/sig1")
                fh = p.fence("kfm/", "vtm/")
                p.seed(fh, "wgu/0", "wgu/1",
                       *[f"hbuf/{ic}/{tg}" for ic in range(8) for tg in (0, 1)])
            else:
                fk, fh = {}, {}
            first_acc = not routed
            slg = [0]
            for m in range(nmega):
                for ic in range(8):
                    icg = m * 8 + ic
                    pg, pu = slg[0] % 2, (slg[0] + 1) % 2
                    p.op("gpsimd", lambda e, s=wg_h[icg, :, :, :], pg=pg:
                         e.dma_start(out=wgu[:, pg, :, :], in_=s),
                         writes=(f"wgu/{pg}",), dma=f"wgu{pg}")
                    p.op("gpsimd", lambda e, s=wu_h[icg, :, :, :], pu=pu:
                         e.dma_start(out=wgu[:, pu, :, :], in_=s),
                         writes=(f"wgu/{pu}",), dma=f"wgu{pu}")
                    slg[0] += 2
                    for tg in range(2):
                        bg, bu = (0, 1) if tg == 0 else (2, 3)
                        spar = (ic * 2 + tg) % 2
                        rhs_reads = tuple(f"lnfm/{dc}/{tt}" for dc in range(NDC)
                                          for tt in range(tg * 4, tg * 4 + 4))
                        for dc in range(NDC):
                            p.op("tensor", lambda e, dc=dc, pg=pg, bg=bg, tg=tg:
                                 e.matmul(pb[bg][:, :], wgu[:, pg, dc, :],
                                          lnfm[:, dc, tg * 512:(tg + 1) * 512],
                                          start=(dc == 0), stop=(dc == NDC - 1)),
                                 reads=(f"wgu/{pg}",) + rhs_reads,
                                 writes=(f"ps/{bg}",))
                        for dc in range(NDC):
                            p.op("tensor", lambda e, dc=dc, pu=pu, bu=bu, tg=tg:
                                 e.matmul(pb[bu][:, :], wgu[:, pu, dc, :],
                                          lnfm[:, dc, tg * 512:(tg + 1) * 512],
                                          start=(dc == 0), stop=(dc == NDC - 1)),
                                 reads=(f"wgu/{pu}",) + rhs_reads,
                                 writes=(f"ps/{bu}",))
                        p.op("scalar", lambda e, bg=bg, spar=spar: e.activation(
                             s_sig[:, spar, :], pb[bg][:], AF.Sigmoid),
                             reads=(f"ps/{bg}",), writes=(f"scr/sig{spar}",))
                        p.op("vector", lambda e, bg=bg, spar=spar: e.tensor_mul(
                             s_sig[:, spar, :], s_sig[:, spar, :], pb[bg][:]),
                             reads=(f"ps/{bg}", f"scr/sig{spar}"),
                             writes=(f"scr/sig{spar}",))
                        p.op("vector", lambda e, bu=bu, spar=spar, ic=ic, tg=tg:
                             e.tensor_mul(h_buf[:, ic, tg * 512:(tg + 1) * 512],
                                          s_sig[:, spar, :], pb[bu][:]),
                             reads=(f"ps/{bu}", f"scr/sig{spar}"),
                             writes=(f"hbuf/{ic}/{tg}",))
                for ogq in range(4):
                    par = (m * 4 + ogq) % 2
                    p.op("gpsimd", lambda e, s=dw_h[m][ogq] if not routed else dw_h[ogq], par=par:
                         e.dma_start(out=wodw[:, par, :, :], in_=s),
                         writes=(f"wodw/{par}",), dma=f"wodw{par}")
                    hreads = tuple(f"hbuf/{ic}/{tg}" for ic in range(8)
                                   for tg in range(2))
                    for tt in range(NTT):
                        bank = 4 + (tt % 2)
                        for ic in range(8):
                            p.op("tensor", lambda e, ic=ic, tt=tt, par=par, bank=bank:
                                 e.matmul(pb[bank][:, :],
                                          h_buf[:, ic, tt * 128:(tt + 1) * 128],
                                          wodw[:, par, ic, :],
                                          start=(ic == 0), stop=(ic == 7)),
                                 reads=(f"wodw/{par}",) + hreads,
                                 writes=(f"ps/{bank}",))
                        dst = xa[:, tt, ogq * 512:(ogq + 1) * 512]
                        if first_acc and m == 0:
                            p.op("vector", lambda e, dst=dst, bank=bank:
                                 e.tensor_copy(dst, pb[bank][:]),
                                 reads=(f"ps/{bank}",), writes=(f"xa/{tt}",))
                        elif not routed:
                            p.op("vector", lambda e, dst=dst, bank=bank:
                                 e.tensor_add(dst, dst, pb[bank][:]),
                                 reads=(f"ps/{bank}", f"xa/{tt}"),
                                 writes=(f"xa/{tt}",))
                        else:
                            p.op("vector", lambda e, dst=dst, bank=bank, tt=tt:
                                 e.scalar_tensor_tensor(
                                     dst, pb[bank][:], comb[:, tt:tt + 1], dst,
                                     ALU.mult, ALU.add),
                                 reads=(f"ps/{bank}", f"comb/{tt}", f"xa/{tt}"),
                                 writes=(f"xa/{tt}",))

        def phase_ffn(c):
            dwq = [[dr["dwm"][m, ogq, :, :, :] for ogq in range(4)]
                   for m in range(8)]
            mlp(c, 8, dr["gw"], dr["uw"], dwq, routed=False)
            rdq = [dr["rd"][ogq, :, :, :] for ogq in range(4)]
            mlp(c, 1, dr["rg"], dr["ru"], rdq, routed=True)

        def phase_finish(c):
            fk = p.fence("scr/")
            p.seed(fk, "scr/big1", "scr/big2")
            def cl(base, tt):
                return cols[:, base + tt:base + tt + 1]
            for tt in range(NTT):
                p.op("vector", lambda e, tt=tt: e.reduce_sum(
                     cl(0, tt), xa[:, tt, :], axis=AX.X),
                     reads=(f"xa/{tt}",), writes=(f"cl/s{tt}",), lazy=True)
            for tt in range(NTT):
                p.op("scalar", lambda e, tt=tt: e.activation(
                     s_lnsq[:], xa[:, tt, :], AF.Square, accum_out=cl(8, tt)),
                     reads=(f"xa/{tt}",), writes=("scr/big1", f"cl/q{tt}"),
                     lazy=True)
            for tt in range(NTT):
                p.op("vector", lambda e, tt=tt: e.tensor_scalar_mul(
                     cl(16, tt), cl(0, tt), 1.0 / D),
                     reads=(f"cl/s{tt}",), writes=(f"cl/m{tt}",))
            for tt in range(NTT):
                p.op("scalar", lambda e, tt=tt: e.activation(
                     cl(24, tt), cl(16, tt), AF.Square),
                     reads=(f"cl/m{tt}",), writes=(f"cl/m2{tt}",))
            for tt in range(NTT):
                p.op("vector", lambda e, tt=tt: e.scalar_tensor_tensor(
                     cl(32, tt), cl(8, tt), 1.0 / D, cl(24, tt),
                     ALU.mult, ALU.subtract),
                     reads=(f"cl/q{tt}", f"cl/m2{tt}"), writes=(f"cl/v{tt}",))
            for tt in range(NTT):
                p.op("scalar", lambda e, tt=tt: e.activation(
                     cl(40, tt), cl(32, tt), AF.Sqrt, bias=cstc[:, 0:1]),
                     reads=(f"cl/v{tt}", "c/all"), writes=(f"cl/sr{tt}",))
            for tt in range(NTT):
                p.op("vector", lambda e, tt=tt: e.reciprocal(
                     cl(48, tt), cl(40, tt)),
                     reads=(f"cl/sr{tt}",), writes=(f"cl/r{tt}",), lazy=True)
            for tt in range(NTT):
                r0 = c * CT + tt * 128
                p.op("sync", lambda e, r0=r0: e.dma_start(
                     out=s_xstg[:], in_=out_h[r0:r0 + 128, :]),
                     reads=(f"out/{c}/{tt}",), writes=("scr/big2",), dma="xg")
                p.op("vector", lambda e, tt=tt: e.tensor_scalar(
                     s_tmp8[:], xa[:, tt, :], cl(16, tt), cl(48, tt),
                     ALU.subtract, ALU.mult),
                     reads=(f"xa/{tt}", f"cl/m{tt}", f"cl/r{tt}"),
                     writes=("scr/big1",))
                p.op("vector", lambda e: e.tensor_add(
                     s_tmp8[:], s_tmp8[:], s_xstg[:]),
                     reads=("scr/big1", "scr/big2"), writes=("scr/big1",))
                p.op("sync", lambda e, r0=r0: e.dma_start(
                     out=out_h[r0:r0 + 128, :], in_=s_tmp8[:]),
                     reads=("scr/big1",), writes=(f"out/{c}/{tt}",), dma="st")

        for c in range(nchunk):
            for tt in range(NTT):
                r0 = c * CT + tt * 128
                p.op("sync", lambda e, r0=r0: e.dma_start(
                     out=xa[:, (r0 // 128) % NTT, :], in_=x_h[r0:r0 + 128, :]),
                     writes=(f"xa/{tt}",), dma=f"x{tt}")
            p.op("sync", lambda e, c=c: e.dma_start(
                 out=cst2[:, :, :], in_=dr["cstab"][:, :, c * CT:(c + 1) * CT]),
                 writes=("c/cos",), dma="cs")
            phase_ln(c, dump=dbg_h.get("lnfm") if c == 0 else None)
            phase_qkv(c)
            if "qfm" in dbg_h and c == 0:
                p.op("sync", lambda e: e.dma_start(out=dbg_h["qfm"], in_=qa[:]),
                     reads=tuple(f"qa/{h}/{qt}" for h in range(NH)
                                 for qt in range(NTT)), dma="dbg")
            if "kfm" in dbg_h and c == 0:
                p.op("sync", lambda e: e.dma_start(out=dbg_h["kfm"], in_=k_fm[:]),
                     reads=tuple(f"kfm/{kv}/{t}" for kv in range(NKV)
                                 for t in range(NTT)), dma="dbg")
            if "vtm" in dbg_h and c == 0:
                p.op("sync", lambda e: e.dma_start(out=dbg_h["vtm"], in_=v_tm[:]),
                     reads=tuple(f"vtm/{t}" for t in range(1 + NTT)), dma="dbg")
            phase_wo_prefetch(c)
            phase_attn(c)
            if "adbg" in dbg_h and c == 0:
                p.op("sync", lambda e: e.dma_start(
                     out=dbg_h["adbg"][:, :, 0:256], in_=s_sc[:, 0:2, :]),
                     reads=("scr/sc0", "scr/sc1", "scr/sc2", "scr/sc3"), dma="dbg")
                p.op("sync", lambda e: e.dma_start(
                     out=dbg_h["adbg"][:, :, 256:512], in_=s_e[:, 0:2, :]),
                     reads=("scr/e0", "scr/e1", "scr/e2", "scr/e3"), dma="dbg")
                p.op("sync", lambda e: e.dma_start(
                     out=dbg_h["adbg"][:, :, 512:544],
                     in_=cols[:, 0:64].rearrange("p (a b) -> p a b", a=2)),
                     reads=tuple(f"cl/a{i}/{j}" for i in range(6)
                                 for j in range(3)), dma="dbg")
                if "adbg2" in dbg_h:
                    p.op("sync", lambda e: e.dma_start(
                         out=dbg_h["adbg2"][:, :, 0:256], in_=s_ebf[:, 0:2, :]),
                         reads=tuple(f"scr/ebf{i}" for i in range(6)), dma="dbg")
                    p.op("sync", lambda e: e.dma_start(
                         out=dbg_h["adbg2"][:, :, 256:512], in_=s_aT[:]),
                         reads=("scr/aT0", "scr/aT1"), dma="dbg")
            if "attno" in dbg_h and c == 0:
                p.op("sync", lambda e: e.dma_start(out=dbg_h["attno"], in_=qa[:]),
                     reads=tuple(f"qa/{h}/{qt}" for h in range(NH)
                                 for qt in range(NTT)), dma="dbg")
            phase_oproj_ln1(c)
            if "xmid" in dbg_h and c == 0:
                p.op("sync", lambda e: e.dma_start(
                     out=dbg_h["xmid"].rearrange("(a p) d -> p a d", p=128),
                     in_=xa[:]),
                     reads=tuple(f"xa/{tt}" for tt in range(NTT)), dma="dbg")
            phase_ln(c)
            phase_spill(c)
            phase_router(c)
            if "comb" in dbg_h and c == 0:
                p.op("sync", lambda e: e.dma_start(out=dbg_h["comb"], in_=comb[:]),
                     reads=tuple(f"comb/{tt}" for tt in range(NTT)), dma="dbg")
            phase_ffn(c)
            if "oacc" in dbg_h and c == 0:
                p.op("sync", lambda e: e.dma_start(out=dbg_h["oacc"], in_=xa[:]),
                     reads=tuple(f"xa/{tt}" for tt in range(NTT)), dma="dbg")
            phase_finish(c)

        global _LAST_PLAN
        _LAST_PLAN = p
        if "mdbg" in dbg_h:
            p.op("sync", lambda e: e.dma_start(
                 out=dbg_h["mdbg"][:, 0:256], in_=mkeep),
                 reads=("c/mkeep",), dma="dbg")
            p.op("sync", lambda e: e.dma_start(
                 out=dbg_h["mdbg"][:, 256:512], in_=mfill),
                 reads=("c/all",), dma="dbg")
        finals = [(p.semh[nm], cval) for nm, cval in p.cnt.items()
                  if nm.startswith("d_") and cval > 0]
        def final_wait(e, finals=finals):
            for h, v in finals:
                e.wait_ge(h, v)
        p.raw("sync", final_wait)

        block = E(nc.Block())
        for eng in ENGS:
            ops = p.q[eng]
            if not ops:
                continue
            getattr(block, eng)(lambda e, ops=ops: [f(e) for f in ops])

    return nc



def kernel(**inputs):
    shared = _host_prep(inputs)
    x = np.ascontiguousarray(np.asarray(inputs["L0_attn_x"], np.float32))
    nc = build(NCHUNK_FULL)
    in_maps = []
    for ci in range(8):
        m = dict(shared)
        m["x"] = np.ascontiguousarray(x[ci])
        in_maps.append(m)
    res = run_bass_kernel_spmd(nc, in_maps, core_ids=list(range(8)))
    return np.stack([r["out"] for r in res.results], axis=0)


# version 08 (final)
# speedup vs baseline: 1.2076x; 1.2076x over previous
import contextlib

import numpy as np
import ml_dtypes

import concourse.bass as bass
import concourse.mybir as mybir
from concourse.bass_utils import run_bass_kernel_spmd

F32 = mybir.dt.float32
BF16 = mybir.dt.bfloat16
AF = mybir.ActivationFunctionType
ALU = mybir.AluOpType
AX = mybir.AxisListType

B, S, D = 8, 4096, 2048
NCHUNK_FULL = 4
CT = 1024
NTT = CT // 128
NH, NKV, DK = 16, 4, 128
INTER, RINTER = 8192, 1024
LN_EPS = 1e-5
RMS_EPS = 1e-8
ROPE_BASE = 10000.0
MASK_FILL = -10000.0
NDC = D // 128

_LAST_PLAN = None
ENGS = ("sync", "tensor", "vector", "scalar", "gpsimd")
ESEM = {"tensor": "s_pe", "vector": "s_dve", "scalar": "s_act",
        "gpsimd": "s_pool", "sync": "s_sp"}
DSEM = {"sync": "d_sp", "scalar": "d_act", "gpsimd": "d_pool"}


class Plan:

    def __init__(self, nc, enter):
        self.nc = nc
        self._enter = enter
        self.q = {e: [] for e in ENGS}
        self.cnt = {}
        self.semh = {}
        self.clock = {e: {} for e in ENGS}
        self.snap = {}
        self.res = {}
        self.selfw = {}
        self.log = []
        for name in set(ESEM.values()) | set(DSEM.values()):
            self.semh[name] = enter(nc.semaphore(name))
            self.cnt[name] = 0

    def _require(self, need, s, c):
        if c > need.get(s, 0):
            need[s] = c

    def op(self, eng, fn, reads=(), writes=(), dma=False, after=None,
           lazy=False):
        if not lazy:
            lazy = any(w.startswith("cl/") or w == "scr/rms" for w in writes)
        if dma:
            assert isinstance(dma, str), "dma= must name a stream class"
            sem = "d_" + dma
            if sem not in self.semh:
                self.semh[sem] = self._enter(self.nc.semaphore(sem))
                self.cnt[sem] = 0
        else:
            sem = ESEM[eng]
        inc = 16 if dma else 1
        need = {}
        force = {}
        for r in reads:
            rr = self.res.get(r)
            if rr and rr["w"]:
                self._require(need, *rr["w"])
                if rr.get("lazy"):
                    self._require(force, *rr["w"])
        for w in writes:
            rr = self.res.get(w)
            if rr:
                if rr["w"]:
                    self._require(need, *rr["w"])
                for s, c in rr["r"]:
                    self._require(need, s, c)
        if after:
            for s, c in after.items():
                self._require(need, s, c)
        clk = self.clock[eng]
        own = ESEM[eng]
        wl = []
        for s, c in need.items():
            if s == own:
                if (not dma) and force.get(s, 0) >= c and self.selfw.get(eng, 0) < c:
                    wl.append((s, c))
                    self.selfw[eng] = c
                continue
            if clk.get(s, 0) >= c:
                continue
            wl.append((s, c))
            snap = self.snap.get(s, {}).get(c)
            if snap:
                for s2, c2 in snap.items():
                    if clk.get(s2, 0) < c2:
                        clk[s2] = c2
            if clk.get(s, 0) < c:
                clk[s] = c
        self.cnt[sem] += inc
        cnt = self.cnt[sem]
        clk[sem] = cnt
        self.snap.setdefault(sem, {})[cnt] = dict(clk)
        semh = self.semh[sem]
        whl = [(self.semh[s], c) for s, c in wl]

        def closure(e, fn=fn, whl=whl, semh=semh, inc=inc):
            for h, c in whl:
                e.wait_ge(h, c)
            ins = fn(e)
            ins.then_inc(semh, inc)

        self.q[eng].append(closure)
        self.log.append((eng, sem, cnt, tuple(wl), tuple(reads), tuple(writes)))
        for r in reads:
            self.res.setdefault(r, {"w": None, "r": []})["r"].append((sem, cnt))
        for w in writes:
            rr = self.res.setdefault(w, {"w": None, "r": []})
            rr["w"] = (sem, cnt)
            rr["r"] = []
            rr["lazy"] = lazy
        return (sem, cnt)

    def fence(self, *prefixes):
        mark = {}
        for name, rr in list(self.res.items()):
            if any(name.startswith(p) for p in prefixes):
                if rr["w"]:
                    self._require(mark, *rr["w"])
                for s, c in rr["r"]:
                    self._require(mark, s, c)
                del self.res[name]
        return mark

    def seed(self, mark, *names):
        for n in names:
            self.res[n] = {"w": None, "r": list(mark.items())}

    def raw(self, eng, fn):
        self.q[eng].append(fn)



_PERM = np.concatenate([np.arange(0, 128, 2), np.arange(1, 128, 2)])


def _tile_lhs(wt, n_oc):
    d, o = wt.shape
    arr = wt.reshape(d // 128, 128, n_oc, o // n_oc)
    return np.ascontiguousarray(arr.transpose(2, 1, 0, 3))


def _host_prep(inp):
    f = np.float32
    wq = inp["L0_attn_Wq"].astype(f)
    wk = inp["L0_attn_Wk"].astype(f)
    wv = inp["L0_attn_Wv"].astype(f)
    wo = inp["L0_attn_Wo"].astype(f)
    wq_p = wq.reshape(NH, DK, D)[:, _PERM, :].reshape(NH * DK, D)
    wk_p = wk.reshape(NKV, DK, D)[:, _PERM, :].reshape(NKV * DK, D)
    bq_p = inp["L0_attn_bq"].astype(f).reshape(NH, DK)[:, _PERM]
    bk_p = inp["L0_attn_bk"].astype(f).reshape(NKV, DK)[:, _PERM]

    out = {}
    out["wq"] = _tile_lhs(np.ascontiguousarray(wq_p.T), NH)
    out["wk"] = _tile_lhs(np.ascontiguousarray(wk_p.T), NKV)
    out["wv"] = _tile_lhs(np.ascontiguousarray(wv.T), NKV)
    wot = np.ascontiguousarray(wo.T)
    out["wo"] = np.ascontiguousarray(
        wot.reshape(NDC, 128, 4, 512).transpose(2, 1, 0, 3))
    gwt = np.ascontiguousarray(inp["L0_ffn_sh_gate"].astype(f).T)
    uwt = np.ascontiguousarray(inp["L0_ffn_sh_up"].astype(f).T)
    out["gw"] = _tile_lhs(gwt, INTER // 128)
    out["uw"] = _tile_lhs(uwt, INTER // 128)
    dwt = np.ascontiguousarray(inp["L0_ffn_sh_down"].astype(f).T)
    out["dwm"] = np.ascontiguousarray(
        dwt.reshape(8, 8, 128, 4, 512).transpose(0, 3, 2, 1, 4))
    rgt = np.ascontiguousarray(inp["L0_ffn_r_gate"].astype(f).T)
    rut = np.ascontiguousarray(inp["L0_ffn_r_up"].astype(f).T)
    out["rg"] = _tile_lhs(rgt, RINTER // 128)
    out["ru"] = _tile_lhs(rut, RINTER // 128)
    rdt = np.ascontiguousarray(inp["L0_ffn_r_down"].astype(f).T)
    out["rd"] = np.ascontiguousarray(
        rdt.reshape(8, 128, 4, 512).transpose(2, 1, 0, 3))
    out["rw"] = np.ascontiguousarray(
        inp["L0_ffn_router_weight"].astype(f).T.reshape(NDC, 128, 8))

    out["bqc"] = np.ascontiguousarray(bq_p.T)
    out["bkc"] = np.ascontiguousarray(bk_p.T)
    out["bvc"] = np.ascontiguousarray(
        inp["L0_attn_bv"].astype(f).reshape(NKV, DK).T)
    out["bob"] = np.ascontiguousarray(
        np.broadcast_to(inp["L0_attn_bo"].astype(f)[None, :], (128, D)))
    out["qnc"] = np.ascontiguousarray(
        inp["L0_attn_qn_w"].astype(f)[_PERM].reshape(128, 1))
    out["knc"] = np.ascontiguousarray(
        inp["L0_attn_kn_w"].astype(f)[_PERM].reshape(128, 1))
    out["sclc"] = np.ascontiguousarray(
        np.broadcast_to(np.asarray(inp["L0_attn_scale"], f).reshape(1, 1), (128, 1)))

    j = np.arange(0, DK, 2, dtype=np.float64) / DK
    inv_freq = (1.0 / (ROPE_BASE ** j)).astype(f)
    t = np.arange(S, dtype=f)
    theta = (t[:, None] * inv_freq[None, :]).astype(f)
    cos_t = np.cos(theta.astype(np.float64)).astype(f).T
    sin_t = np.sin(theta.astype(np.float64)).astype(f).T
    cstab = np.stack([np.concatenate([cos_t, cos_t], axis=0),
                      np.concatenate([-sin_t, sin_t], axis=0)], axis=1)
    out["cstab"] = np.ascontiguousarray(cstab.transpose(0, 1, 2)) \
        if cstab.shape == (128, 2, 4096) else np.ascontiguousarray(cstab)
    pm = np.zeros((128, 128), f)
    pm[np.arange(128), (np.arange(128) + 64) % 128] = 1.0
    out["pshift"] = pm

    qi = np.arange(128)[:, None]
    ki = np.arange(128)[None, :]
    keep_prev = (ki >= qi + 64).astype(f)
    keep_diag = ((qi - ki >= 0) & (qi - ki <= 64)).astype(f)
    keep = np.concatenate([keep_prev, keep_diag], axis=1)
    out["mkeep"] = np.ascontiguousarray(keep)
    out["mfill"] = np.ascontiguousarray((1.0 - keep) * MASK_FILL)
    out["mk8"] = np.ascontiguousarray(keep.astype(np.uint8))
    pm = out.pop("pshift")
    cpk = np.zeros((128, 928), f)
    cpk[:, 0:256] = out.pop("mkeep")
    cpk[:, 256:512] = out.pop("mfill")
    cpk[:, 512:640] = 1.0
    cpk[:, 640:768] = pm
    cpk[:, 768:896] = np.eye(128, dtype=f)
    cpk[:, 896:912] = out.pop("bqc")
    cpk[:, 912:916] = out.pop("bkc")
    cpk[:, 916:920] = out.pop("bvc")
    cpk[:, 920:921] = out.pop("qnc")
    cpk[:, 921:922] = out.pop("knc")
    cpk[:, 922:923] = out.pop("sclc")
    cpk[:, 923:925] = np.array([LN_EPS, RMS_EPS], f)[None, :]
    out["cpk"] = np.ascontiguousarray(cpk)
    return out



def build(nchunk=NCHUNK_FULL, dbg=()):
    nc = bass.Bass("TRN2", target_bir_lowering=False, debug=False)

    dr = {}
    def din(name, shape, dtype=F32):
        dr[name] = nc.dram_tensor(name, list(shape), dtype, kind="ExternalInput").ap()
        return dr[name]

    x_h = din("x", (S, D))
    out_h = nc.dram_tensor("out", [S, D], F32, kind="ExternalOutput").ap()
    din("wq", (NH, 128, NDC, 128)); din("wk", (NKV, 128, NDC, 128))
    din("wv", (NKV, 128, NDC, 128)); din("wo", (4, 128, NDC, 512))
    din("gw", (64, 128, NDC, 128)); din("uw", (64, 128, NDC, 128))
    din("dwm", (8, 4, 128, 8, 512))
    din("rg", (8, 128, NDC, 128)); din("ru", (8, 128, NDC, 128))
    din("rd", (4, 128, 8, 512)); din("rw", (NDC, 128, 8))
    din("bob", (128, D)); din("cstab", (128, 2, S))
    din("cpk", (128, 928)); din("mk8", (128, 256), mybir.dt.uint8)

    bft = {}
    for nm, shape in [("wq", (NH, 128, NDC, 128)), ("wk", (NKV, 128, NDC, 128)),
                      ("wv", (NKV, 128, NDC, 128)), ("wo", (4, 128, NDC, 512)),
                      ("gw", (64, 128, NDC, 128)), ("uw", (64, 128, NDC, 128)),
                      ("dwm", (8, 4, 128, 8, 512)), ("rg", (8, 128, NDC, 128)),
                      ("ru", (8, 128, NDC, 128)), ("rd", (4, 128, 8, 512))]:
        bft[nm] = nc.dram_tensor("bf_" + nm, list(shape), BF16,
                                 kind="Internal").ap()
    MCLS = {"wA": "mA", "wgu": "mG", "wodw": "mD"}
    mirror_tot = {}

    dbg_h = {}
    for name, (shape, dt) in dict(
        lnfm=((128, NDC, CT), BF16), qfm=((128, NH, CT), BF16),
        kfm=((128, NKV, 128 + CT), BF16), vtm=((128, 1 + NTT, 512), BF16),
        attno=((128, NH, CT), BF16), xmid=((CT, D), F32),
        comb=((128, NTT), F32), hbuf=((128, 8, CT), BF16),
        oacc=((128, NTT, D), F32), adbg=((128, 2, 256 * 4), F32),
        adbg2=((128, 2, 512), BF16), mdbg=((128, 512), F32),
    ).items():
        if name in dbg:
            dbg_h[name] = nc.dram_tensor(
                "dbg_" + name, list(shape), dt, kind="ExternalOutput").ap()

    with contextlib.ExitStack() as stack:
        E = stack.enter_context
        A = nc.alloc_sbuf_tensor

        xa = A("xa", [128, NTT, D], F32)
        lnfm = A("lnfm", [128, NDC, CT], BF16)
        qa = A("qa", [128, NH, CT], BF16)
        rkv = A("rkv", [128, 24576 // 2], BF16)
        kvaddr = nc.lookup_mloc(rkv).addr
        AT = nc.alloc_sbuf_tensor_at
        k_fm = AT("k_fm", [128, NKV, 128 + CT], BF16, offset=kvaddr)
        v_tm = AT("v_tm", [128, 1 + NTT, 512], BF16, offset=kvaddr + 9216)
        h_buf = AT("h_buf", [128, 8, CT], BF16, offset=kvaddr)
        wgu = AT("wgu", [128, 2, NDC, 128], BF16, offset=kvaddr + 16384)
        wodw = A("wodw", [128, 2, 8, 512], BF16)
        wA = A("wA", [128, 2, 8, 128], BF16)

        scr = A("scr", [128, 20480 // 4], F32)
        sa = nc.lookup_mloc(scr).addr
        s_sq = AT("s_sq", [128, 2, 512], F32, offset=sa)
        s_qraw = AT("s_qraw", [128, 2, 512], F32, offset=sa + 4096)
        s_qn = AT("s_qn", [128, 512], F32, offset=sa + 8192)
        s_rstd = AT("s_rstd", [128, 512], F32, offset=sa + 10240)
        s_rope = AT("s_rope", [128, 2, 512], F32, offset=sa + 12288)
        s_rms = AT("s_rms", [1, 512], F32, offset=sa + 16384)
        s_vfm = AT("s_vfm", [128, CT], BF16, offset=sa + 18432)
        s_lntm = AT("s_lntm", [128, D], BF16, offset=sa)
        s_big1 = AT("s_big1", [128, D], F32, offset=sa + 4096)
        s_big2 = AT("s_big2", [128, D], F32, offset=sa + 12288)
        s_sc = AT("s_sc", [128, 4, 256], F32, offset=sa)
        s_e = AT("s_e", [128, 4, 256], F32, offset=sa + 4096)
        s_ebf = AT("s_ebf", [128, 6, 256], BF16, offset=sa + 8192)
        s_aT = AT("s_aT", [128, 2, 256], BF16, offset=sa + 11264)
        s_sig = AT("s_sig", [128, 2, 512], F32, offset=sa)
        s_lnsq = s_big1
        s_bo = s_big2
        s_l1t = s_big1
        s_xstg = s_big2
        s_tmp8 = s_big1

        cst2 = A("cst2", [128, 2, CT], F32)
        ct = cst2[:, 0, :]
        st = cst2[:, 1, :]
        cpk = A("cpk_s", [128, 928], F32)
        mkeep = cpk[:, 0:256]
        mfill = cpk[:, 256:512]
        ones = cpk[:, 512:640]
        psh = cpk[:, 640:768]
        identf = cpk[:, 768:896]
        bqc = cpk[:, 896:912]
        bkc = cpk[:, 912:916]
        bvc = cpk[:, 916:920]
        qnc = cpk[:, 920:921]
        knc = cpk[:, 921:922]
        sclc = cpk[:, 922:923]
        cstc = cpk[:, 923:925]
        ident_t = A("ident_s", [128, 128], BF16)
        ident = ident_t[:, :]
        mk8 = A("mk8_s", [128, 256], mybir.dt.uint8)
        qscl = A("qscl_s", [128, 1], F32)
        rw_sb = A("rw_sb", [128, NDC, 8], BF16)
        comb = A("comb_s", [128, NTT], F32)
        cols = A("cols_s", [128, 80], F32)
        k_halo = A("k_halo", [128, NKV, 128], BF16)
        v_halo = A("v_halo", [128, 512], BF16)

        oproj = AT("oproj", [128, 4, D], F32, offset=nc.lookup_mloc(lnfm).addr)

        pb = [E(nc.psum_tensor(f"pb{i}", [128, 512], F32)) for i in range(6)]
        pbT = [E(nc.psum_tensor(f"pbT{i}", [128, 1024], BF16)) for i in range(2)]

        p = Plan(nc, E)

        def col(par, i):
            return cols[:, par * 32 + i:par * 32 + i + 1]

        p.op("sync", lambda e: e.dma_start(out=cpk[:, :], in_=dr["cpk"]),
             writes=("c/all",), dma="cpk")
        p.op("sync", lambda e: e.dma_start(out=mk8[:], in_=dr["mk8"]),
             writes=("c/mk8",), dma="mk8")
        p.op("vector", lambda e: e.tensor_mul(qscl[:], qnc, sclc),
             reads=("c/all",), writes=("c/qscl",))
        p.op("vector", lambda e: e.tensor_copy(ident_t[:], identf),
             reads=("c/all",), writes=("c/ident16",))
        p.op("gpsimd", lambda e: e.dma_start(
            out=rw_sb[:], in_=dr["rw"].rearrange("dc p o -> p dc o")),
            writes=("c/rw",), dma="rw")


        def phase_ln(c, dump=None):
            fk = p.fence("scr/")
            p.seed(fk, "scr/big1", "scr/lntm")
            def cl(base, tt):
                return cols[:, base + tt:base + tt + 1]
            for grp in range(2):
                tts = range(grp * 4, grp * 4 + 4)
                for tt in tts:
                    p.op("vector", lambda e, tt=tt: e.reduce_sum(
                         cl(0, tt), xa[:, tt, :], axis=AX.X),
                         reads=(f"xa/{tt}",), writes=(f"cl/s{tt}",), lazy=True)
                for tt in tts:
                    p.op("scalar", lambda e, tt=tt: e.activation(
                         s_lnsq[:], xa[:, tt, :], AF.Square, accum_out=cl(8, tt)),
                         reads=(f"xa/{tt}",), writes=("scr/big1", f"cl/q{tt}"),
                         lazy=True)
                for tt in tts:
                    p.op("vector", lambda e, tt=tt: e.tensor_scalar_mul(
                         cl(16, tt), cl(0, tt), 1.0 / D),
                         reads=(f"cl/s{tt}",), writes=(f"cl/m{tt}",))
                for tt in tts:
                    p.op("scalar", lambda e, tt=tt: e.activation(
                         cl(24, tt), cl(16, tt), AF.Square),
                         reads=(f"cl/m{tt}",), writes=(f"cl/m2{tt}",))
                for tt in tts:
                    p.op("vector", lambda e, tt=tt: e.scalar_tensor_tensor(
                         cl(32, tt), cl(8, tt), 1.0 / D, cl(24, tt),
                         ALU.mult, ALU.subtract),
                         reads=(f"cl/q{tt}", f"cl/m2{tt}"), writes=(f"cl/v{tt}",))
                for tt in tts:
                    p.op("scalar", lambda e, tt=tt: e.activation(
                         cl(40, tt), cl(32, tt), AF.Sqrt, bias=cstc[:, 0:1]),
                         reads=(f"cl/v{tt}", "c/all"), writes=(f"cl/sr{tt}",))
                for tt in tts:
                    p.op("vector", lambda e, tt=tt: e.reciprocal(
                         cl(48, tt), cl(40, tt)),
                         reads=(f"cl/sr{tt}",), writes=(f"cl/r{tt}",), lazy=True)
                for tt in tts:
                    p.op("vector", lambda e, tt=tt: e.tensor_scalar(
                         s_lntm[:], xa[:, tt, :], cl(16, tt), cl(48, tt),
                         ALU.subtract, ALU.mult),
                         reads=(f"xa/{tt}", f"cl/m{tt}", f"cl/r{tt}"),
                         writes=("scr/lntm",))
                    for half in range(2):
                        tb = (tt * 2 + half) % 2
                        for j in range(8):
                            dc = half * 8 + j
                            p.op("tensor", lambda e, dc=dc, j=j, tb=tb: e.transpose(
                                 pbT[tb][:, j * 128:(j + 1) * 128],
                                 s_lntm[:, dc * 128:(dc + 1) * 128], ident),
                                 reads=("scr/lntm", "c/ident16"), writes=(f"psT/{tb}",))
                        p.op("scalar", lambda e, tt=tt, half=half, tb=tb: e.copy(
                             lnfm[:, half * 8:(half + 1) * 8, tt * 128:(tt + 1) * 128],
                             pbT[tb].rearrange("p (a b) -> p a b", a=8)),
                             reads=(f"psT/{tb}",),
                             writes=tuple(f"lnfm/{half*8+j}/{tt}" for j in range(8)))
            if dump is not None:
                p.op("sync", lambda e: e.dma_start(out=dump, in_=lnfm[:]),
                     reads=tuple(f"lnfm/{dc}/{tt}" for dc in range(NDC)
                                 for tt in range(NTT)), dma="dbg")

        def stream_slab(c, par, kind, name, slicer):
            buf = {"wA": wA, "wgu": wgu, "wodw": wodw}[kind]
            mcls = MCLS[kind]
            if c == 0:
                p.op("gpsimd",
                     lambda e, b=buf, par=par, s=slicer(dr[name]):
                     e.dma_start(out=b[:, par, :, :], in_=s),
                     writes=(f"{kind}/{par}",), dma=f"{kind}{par}")
                p.op("scalar",
                     lambda e, b=buf, par=par, d=slicer(bft[name]):
                     e.dma_start(out=d, in_=b[:, par, :, :]),
                     reads=(f"{kind}/{par}",), dma=mcls)
            else:
                p.op("gpsimd",
                     lambda e, b=buf, par=par, s=slicer(bft[name]):
                     e.dma_start(out=b[:, par, :, :], in_=s),
                     writes=(f"{kind}/{par}",), dma=f"{kind}{par}",
                     after={"d_" + mcls: mirror_tot.get(mcls, 0)})

        def qkv_head(c, oc, tg, wpars, kind, u):
            bank = u % 3
            par = u % 2
            lhs_reads = tuple(f"wA/{wp}" for wp in wpars)
            rhs_reads = tuple(f"lnfm/{dc}/{tt}" for dc in range(NDC)
                              for tt in range(tg * 4, tg * 4 + 4))
            for dc in range(NDC):
                wp = wpars[dc // 8]
                p.op("tensor", lambda e, dc=dc, wp=wp, bank=bank, tg=tg:
                     e.matmul(pb[bank][:, :], wA[:, wp, dc % 8, :],
                              lnfm[:, dc, tg * 512:(tg + 1) * 512],
                              start=(dc == 0), stop=(dc == NDC - 1)),
                     reads=lhs_reads + rhs_reads, writes=(f"ps/{bank}",))
            bias = {"q": bqc, "k": bkc}[kind][:, oc:oc + 1]
            p.op("scalar", lambda e, bank=bank, bias=bias, par=par:
                 e.activation(s_sq[:, par, :], pb[bank][:], AF.Square, bias=bias),
                 reads=(f"ps/{bank}", "c/all"), writes=(f"scr/sq{par}",))
            p.op("scalar", lambda e, bank=bank, bias=bias, par=par:
                 e.activation(s_qraw[:, par, :], pb[bank][:], AF.Identity,
                              bias=bias),
                 reads=(f"ps/{bank}",), writes=(f"scr/qraw{par}",))

        def qkv_tail(c, oc, tg, kind, u):
            par = u % 2
            rot_bank = 3 if u % 2 == 0 else 5
            nw = qscl[:, 0:1] if kind == "q" else knc
            p.op("tensor", lambda e, par=par: e.matmul(
                 pb[4][0:1, :], ones[:, 0:1], s_sq[:, par, :],
                 start=True, stop=True),
                 reads=(f"scr/sq{par}", "c/all"), writes=("ps/4",))
            p.op("scalar", lambda e: e.activation(
                 s_rms[0:1, :], pb[4][0:1, :], AF.Identity, scale=1.0 / DK),
                 reads=("ps/4",), writes=("scr/rms",))
            p.op("tensor", lambda e: e.matmul(
                 pb[4][:, :], ones[0:1, 0:128], s_rms[0:1, :],
                 start=True, stop=True),
                 reads=("scr/rms",), writes=("ps/4",))
            p.op("scalar", lambda e: e.activation(
                 s_rstd[:], pb[4][:, :], AF.Sqrt, bias=cstc[:, 1:2]),
                 reads=("ps/4", "c/all"), writes=("scr/rstd",))
            p.op("vector", lambda e: e.reciprocal(s_rstd[:], s_rstd[:]),
                 reads=("scr/rstd",), writes=("scr/rstd",))
            p.op("vector", lambda e, nw=nw, par=par: e.scalar_tensor_tensor(
                 s_qn[:], s_rstd[:], nw, s_qraw[:, par, :],
                 ALU.mult, ALU.mult),
                 reads=("scr/rstd", f"scr/qraw{par}", "c/all"),
                 writes=("scr/qn",))
            p.op("tensor", lambda e, rot_bank=rot_bank: e.matmul(
                 pb[rot_bank][:, :], psh, s_qn[:], start=True, stop=True),
                 reads=("scr/qn", "c/all"), writes=(f"ps/{rot_bank}",))
            csl = ct[:, tg * 512:(tg + 1) * 512]
            snl = st[:, tg * 512:(tg + 1) * 512]
            if kind == "q":
                dst = qa[:, oc, tg * 512:(tg + 1) * 512]
                dres = tuple(f"qa/{oc}/{qt}" for qt in range(tg * 4, tg * 4 + 4))
            else:
                dst = k_fm[:, oc, 128 + tg * 512:128 + (tg + 1) * 512]
                dres = tuple(f"kfm/{oc}/{qt}" for qt in range(tg * 4, tg * 4 + 4))
            p.op("vector", lambda e, csl=csl: e.tensor_mul(
                 s_rope[:, 0, :], s_qn[:], csl),
                 reads=("scr/qn", "c/cos"), writes=("scr/rope0",))
            p.op("vector", lambda e, snl=snl, rot_bank=rot_bank: e.tensor_mul(
                 s_rope[:, 1, :], pb[rot_bank][:, :], snl),
                 reads=(f"ps/{rot_bank}", "c/cos"), writes=("scr/rope1",))
            p.op("vector", lambda e, dst=dst: e.tensor_add(
                 dst, s_rope[:, 0, :], s_rope[:, 1, :]),
                 reads=("scr/rope0", "scr/rope1"), writes=dres)

        def v_unit(c, oc, wpars):
            bank0, bank1 = 0, 1
            lhs_reads = tuple(f"wA/{wp}" for wp in wpars)
            for tg in range(2):
                bank = (bank0, bank1)[tg]
                rhs_reads = tuple(f"lnfm/{dc}/{tt}" for dc in range(NDC)
                                  for tt in range(tg * 4, tg * 4 + 4))
                for dc in range(NDC):
                    wp = wpars[dc // 8]
                    p.op("tensor", lambda e, dc=dc, wp=wp, bank=bank, tg=tg:
                         e.matmul(pb[bank][:, :], wA[:, wp, dc % 8, :],
                                  lnfm[:, dc, tg * 512:(tg + 1) * 512],
                                  start=(dc == 0), stop=(dc == NDC - 1)),
                         reads=lhs_reads + rhs_reads, writes=(f"ps/{bank}",))
                p.op("scalar", lambda e, bank=bank, tg=tg, oc=oc:
                     e.activation(s_vfm[:, tg * 512:(tg + 1) * 512], pb[bank][:],
                                  AF.Identity, bias=bvc[:, oc:oc + 1]),
                     reads=(f"ps/{bank}", "c/all"), writes=("scr/vfm",))
            for ti in range(NTT):
                tb = ti % 2
                p.op("tensor", lambda e, ti=ti, tb=tb: e.transpose(
                     pbT[tb][:, (ti // 2) * 128:(ti // 2 + 1) * 128],
                     s_vfm[:, ti * 128:(ti + 1) * 128], ident),
                     reads=("scr/vfm", "c/ident16"), writes=(f"psT/{tb}",))
            for tb in range(2):
                p.op("scalar", lambda e, oc=oc, tb=tb: e.copy(
                     v_tm[:, 1 + tb:1 + NTT:2, oc * 128:(oc + 1) * 128],
                     pbT[tb][:, 0:512].rearrange("p (a b) -> p a b", a=4)),
                     reads=(f"psT/{tb}",),
                     writes=tuple(f"vtm/{1+ti}" for ti in range(tb, NTT, 2)))

        def phase_qkv(c):
            fkv = p.fence("hbuf/", "wgu/") if c > 0 else {}
            fscr = p.fence("scr/")
            p.seed(fscr, "scr/sq0", "scr/sq1", "scr/qraw0", "scr/qraw1",
                   "scr/qn", "scr/rstd", "scr/rms", "scr/rope0", "scr/rope1",
                   "scr/vfm")
            if c > 0:
                p.op("gpsimd", lambda e: e.tensor_copy(
                     k_fm[:, :, 0:128], k_halo[:]),
                     reads=("khalo",), writes=tuple(f"kfm/{kv}/halo" for kv in range(NKV)),
                     after=fkv)
                p.op("gpsimd", lambda e: e.tensor_copy(v_tm[:, 0, :], v_halo[:]),
                     reads=("vhalo",), writes=("vtm/0",), after=fkv)
            sl = 0
            def load(name, oc, h8, par):
                nonlocal sl
                r = stream_slab(c, par, "wA", name,
                                lambda t, oc=oc, h8=h8: t[oc, :, h8 * 8:(h8 + 1) * 8, :])
                sl += 1
                return r
            units = ([("q", oc, "wq") for oc in range(NH)]
                     + [("k", oc, "wk") for oc in range(NKV)]
                     + [("v", oc, "wv") for oc in range(NKV)])
            pend = None
            u = 0
            for kind, oc, w in units:
                p0, p1 = (sl % 2), ((sl + 1) % 2)
                load(w, oc, 0, p0)
                load(w, oc, 1, p1)
                if kind == "v":
                    if pend is not None:
                        qkv_tail(c, *pend)
                        pend = None
                    v_unit(c, oc, (p0, p1))
                else:
                    for tg in range(2):
                        qkv_head(c, oc, tg, (p0, p1), kind, u)
                        if pend is not None:
                            qkv_tail(c, *pend)
                        pend = (oc, tg, kind, u)
                        u += 1
            if pend is not None:
                qkv_tail(c, *pend)
            p.op("gpsimd", lambda e: e.tensor_copy(
                 k_halo[:], k_fm[:, :, CT:CT + 128]),
                 reads=tuple(f"kfm/{kv}/{NTT-1}" for kv in range(NKV)),
                 writes=("khalo",))
            p.op("gpsimd", lambda e: e.tensor_copy(v_halo[:], v_tm[:, NTT, :]),
                 reads=(f"vtm/{NTT}",), writes=("vhalo",))

        def attn_head(c, qt, h):
            u = qt * NH + h
            par = u % 4
            par6 = u % 6
            kv = h // 4
            first = (c == 0 and qt == 0)
            N = 128 if first else 256
            moff = 256 - N
            kcol0 = qt * 128 + (128 if first else 0)
            sb = 4 + u % 2
            def acol(j, par6=par6):
                return cols[:, 56 + par6 * 3 + j:56 + par6 * 3 + j + 1]
            kres = (f"kfm/{kv}/halo",) if qt == 0 and not first else ()
            kres = kres + tuple(f"kfm/{kv}/{t}" for t in range(max(0, qt - 1), qt + 1))
            p.op("tensor", lambda e, h=h, qt=qt, kv=kv, N=N, kcol0=kcol0, sb=sb:
                 e.matmul(pb[sb][:, 0:N], qa[:, h, qt * 128:(qt + 1) * 128],
                          k_fm[:, kv, kcol0:kcol0 + N], start=True, stop=True),
                 reads=(f"qa/{h}/{qt}",) + kres, writes=(f"ps/{sb}",))
            p.op("vector", lambda e, par=par, N=N, moff=moff, sb=sb:
                 e.copy_predicated(s_sc[:, par, moff:moff + N],
                                   mk8[:, moff:moff + N], pb[sb][:, 0:N]),
                 reads=(f"ps/{sb}", "c/mk8"), writes=(f"scr/sc{par}",))
            p.op("vector", lambda e, par=par, N=N, moff=moff: e.reduce_max(
                 acol(0), s_sc[:, par, moff:moff + N], axis=AX.X, negate=True),
                 reads=(f"scr/sc{par}",), writes=(f"cl/a{par6}/0",), lazy=True)
            p.op("scalar", lambda e, par=par, N=N, moff=moff: e.activation(
                 s_e[:, par, moff:moff + N], s_sc[:, par, moff:moff + N], AF.Exp,
                 bias=acol(0), accum_out=acol(1)),
                 reads=(f"scr/sc{par}", f"cl/a{par6}/0"),
                 writes=(f"scr/e{par}", f"cl/a{par6}/1"), lazy=True)
            p.op("vector", lambda e: e.reciprocal(acol(2), acol(1)),
                 reads=(f"cl/a{par6}/1",), writes=(f"cl/a{par6}/2",), lazy=True)
            p.op("vector", lambda e, par=par, par6=par6, N=N, moff=moff:
                 e.tensor_scalar_mul(
                     s_ebf[:, par6, moff:moff + N],
                     s_e[:, par, moff:moff + N], acol(2)),
                 reads=(f"scr/e{par}", f"cl/a{par6}/2"),
                 writes=(f"scr/ebf{par6}",))

        def attn_tail(c, qt, h):
            u = qt * NH + h
            par = u % 2
            par6 = u % 6
            kv = h // 4
            first = (c == 0 and qt == 0)
            N = 128 if first else 256
            nb = N // 128
            moff = 256 - N
            for b in range(nb):
                p.op("tensor", lambda e, par=par, par6=par6, b=b, moff=moff:
                     e.transpose(
                         pbT[par][:, b * 128:(b + 1) * 128],
                         s_ebf[:, par6, moff + b * 128:moff + (b + 1) * 128],
                         ident),
                     reads=(f"scr/ebf{par6}", "c/ident16"), writes=(f"psT/{par}",))
            p.op("scalar", lambda e, par=par, N=N: e.copy(
                 s_aT[:, par, 0:N], pbT[par][:, 0:N]),
                 reads=(f"psT/{par}",), writes=(f"scr/aT{par}",))
            vres = () if first else (f"vtm/{qt}",)
            vres = vres + (f"vtm/{qt + 1}",)
            for b in range(nb):
                vt = qt + b + (1 if first else 0)
                p.op("tensor", lambda e, par=par, b=b, vt=vt, kv=kv, nb=nb:
                     e.matmul(pb[par][:, 0:128],
                              v_tm[:, vt, kv * 128:(kv + 1) * 128],
                              s_aT[:, par, b * 128:(b + 1) * 128],
                              start=(b == 0), stop=(b == nb - 1)),
                     reads=(f"scr/aT{par}",) + vres, writes=(f"ps/{par}",))
            p.op("scalar", lambda e, h=h, qt=qt, par=par: e.copy(
                 qa[:, h, qt * 128:(qt + 1) * 128], pb[par][:, 0:128]),
                 reads=(f"ps/{par}",), writes=(f"qa/{h}/{qt}",))

        def phase_attn(c):
            fk = p.fence("scr/")
            p.seed(fk, *[f"scr/sc{par}" for par in range(4)],
                   *[f"scr/e{par}" for par in range(4)],
                   *[f"scr/ebf{par}" for par in range(6)],
                   "scr/aT0", "scr/aT1")
            for par in range(4):
                p.op("vector", lambda e, par=par: e.tensor_copy(
                     s_sc[:, par, :], mfill),
                     reads=("c/all",), writes=(f"scr/sc{par}",))
            from collections import deque
            pend = deque()
            for qt in range(NTT):
                for h in range(NH):
                    attn_head(c, qt, h)
                    if len(pend) >= 4:
                        attn_tail(c, *pend.popleft())
                    pend.append((qt, h))
            while pend:
                attn_tail(c, *pend.popleft())

        def phase_wo_prefetch(c):
            for dh in range(2):
                stream_slab(c, dh % 2, "wodw", "wo",
                            lambda t, dh=dh: t[0, :, dh * 8:(dh + 1) * 8, :])

        def phase_oproj_ln1(c):
            fk = p.fence("scr/", "lnfm/")
            p.seed(fk, "scr/big1", "scr/big2",
                   *[f"oproj/{tq}" for tq in range(4)])
            p.op("sync", lambda e: e.dma_start(out=s_bo[:], in_=dr["bob"]),
                 writes=("scr/big2",), dma="bo")
            sl = 0
            for half in range(2):
                for og in range(4):
                    pars = []
                    for dh in range(2):
                        par = sl % 2
                        if not (half == 0 and og == 0):
                            stream_slab(c, par, "wodw", "wo",
                                        lambda t, og=og, dh=dh:
                                        t[og, :, dh * 8:(dh + 1) * 8, :])
                        pars.append(par)
                        sl += 1
                    for dc in range(NDC):
                        wp = pars[dc // 8]
                        for tq in range(4):
                            tt = half * 4 + tq
                            p.op("tensor",
                                 lambda e, dc=dc, wp=wp, tq=tq, tt=tt:
                                 e.matmul(pb[tq][:, :], qa[:, dc, tt * 128:(tt + 1) * 128],
                                          wodw[:, wp, dc % 8, :],
                                          start=(dc == 0), stop=(dc == NDC - 1)),
                                 reads=(f"qa/{dc}/{tt}", f"wodw/{wp}"),
                                 writes=(f"ps/{tq}",))
                    for tq in range(4):
                        tt = half * 4 + tq
                        p.op("vector", lambda e, tq=tq, og=og: e.tensor_add(
                             oproj[:, tq, og * 512:(og + 1) * 512], pb[tq][:],
                             s_bo[:, og * 512:(og + 1) * 512]),
                             reads=(f"ps/{tq}", "scr/big2"),
                             writes=(f"oproj/{tq}",))
                def cl(base, tq):
                    return cols[:, base + tq:base + tq + 1]
                for tq in range(4):
                    p.op("vector", lambda e, tq=tq: e.reduce_sum(
                         cl(0, tq), oproj[:, tq, :], axis=AX.X),
                         reads=(f"oproj/{tq}",), writes=(f"cl/s{tq}",), lazy=True)
                for tq in range(4):
                    p.op("scalar", lambda e, tq=tq: e.activation(
                         s_lnsq[:], oproj[:, tq, :], AF.Square,
                         accum_out=cl(8, tq)),
                         reads=(f"oproj/{tq}",),
                         writes=("scr/big1", f"cl/q{tq}"), lazy=True)
                for tq in range(4):
                    p.op("vector", lambda e, tq=tq: e.tensor_scalar_mul(
                         cl(16, tq), cl(0, tq), 1.0 / D),
                         reads=(f"cl/s{tq}",), writes=(f"cl/m{tq}",))
                for tq in range(4):
                    p.op("scalar", lambda e, tq=tq: e.activation(
                         cl(24, tq), cl(16, tq), AF.Square),
                         reads=(f"cl/m{tq}",), writes=(f"cl/m2{tq}",))
                for tq in range(4):
                    p.op("vector", lambda e, tq=tq: e.scalar_tensor_tensor(
                         cl(32, tq), cl(8, tq), 1.0 / D, cl(24, tq),
                         ALU.mult, ALU.subtract),
                         reads=(f"cl/q{tq}", f"cl/m2{tq}"), writes=(f"cl/v{tq}",))
                for tq in range(4):
                    p.op("scalar", lambda e, tq=tq: e.activation(
                         cl(40, tq), cl(32, tq), AF.Sqrt, bias=cstc[:, 0:1]),
                         reads=(f"cl/v{tq}", "c/all"), writes=(f"cl/sr{tq}",))
                for tq in range(4):
                    p.op("vector", lambda e, tq=tq: e.reciprocal(
                         cl(48, tq), cl(40, tq)),
                         reads=(f"cl/sr{tq}",), writes=(f"cl/r{tq}",), lazy=True)
                for tq in range(4):
                    tt = half * 4 + tq
                    p.op("vector", lambda e, tq=tq: e.tensor_scalar(
                         s_l1t[:], oproj[:, tq, :], cl(16, tq), cl(48, tq),
                         ALU.subtract, ALU.mult),
                         reads=(f"oproj/{tq}", f"cl/m{tq}", f"cl/r{tq}"),
                         writes=("scr/big1",))
                    p.op("vector", lambda e, tt=tt: e.tensor_add(
                         xa[:, tt, :], xa[:, tt, :], s_l1t[:]),
                         reads=("scr/big1", f"xa/{tt}"), writes=(f"xa/{tt}",))

        def phase_spill(c):
            p.op("sync", lambda e, c=c: e.dma_start(
                 out=out_h[c * CT:(c + 1) * CT, :].rearrange(
                     "(a p) d -> p a d", p=128),
                 in_=xa[:, :, :]),
                 reads=tuple(f"xa/{tt}" for tt in range(NTT)),
                 writes=tuple(f"out/{c}/{tt}" for tt in range(NTT)), dma="sp")

        def phase_router(c):
            fk = p.fence("scr/")
            p.seed(fk, "scr/lg", "scr/eq")
            def cl(base, tt):
                return cols[:, base + tt:base + tt + 1]
            for tt in range(NTT):
                for dc in range(NDC):
                    p.op("tensor", lambda e, dc=dc, tt=tt: e.matmul(
                         pb[4][:, 0:8], lnfm[:, dc, tt * 128:(tt + 1) * 128],
                         rw_sb[:, dc, :], start=(dc == 0), stop=(dc == NDC - 1)),
                         reads=(f"lnfm/{dc}/{tt}", "c/rw"), writes=("ps/4",))
                p.op("scalar", lambda e, tt=tt: e.copy(
                     s_sq[:, 0, tt * 8:(tt + 1) * 8], pb[4][:, 0:8]),
                     reads=("ps/4",), writes=("scr/lg",))
            for tt in range(NTT):
                p.op("vector", lambda e, tt=tt: e.reduce_max(
                     cl(0, tt), s_sq[:, 0, tt * 8:(tt + 1) * 8], axis=AX.X),
                     reads=("scr/lg",), writes=(f"cl/v0{tt}",), lazy=True)
            for tt in range(NTT):
                p.op("vector", lambda e, tt=tt: e.reduce_max(
                     cl(8, tt), s_sq[:, 0, tt * 8:(tt + 1) * 8], axis=AX.X,
                     negate=True),
                     reads=("scr/lg",), writes=(f"cl/nv{tt}",), lazy=True)
            for tt in range(NTT):
                p.op("vector", lambda e, tt=tt: e.tensor_single_scalar(
                     s_qraw[:, 0, tt * 8:(tt + 1) * 8],
                     s_sq[:, 0, tt * 8:(tt + 1) * 8], cl(0, tt), ALU.is_equal),
                     reads=("scr/lg", f"cl/v0{tt}"), writes=("scr/eq",))
            for tt in range(NTT):
                p.op("vector", lambda e, tt=tt: e.scalar_tensor_tensor(
                     s_qraw[:, 0, tt * 8:(tt + 1) * 8],
                     s_qraw[:, 0, tt * 8:(tt + 1) * 8], -1e30,
                     s_sq[:, 0, tt * 8:(tt + 1) * 8], ALU.mult, ALU.add),
                     reads=("scr/eq", "scr/lg"), writes=("scr/eq",))
            for tt in range(NTT):
                p.op("vector", lambda e, tt=tt: e.reduce_max(
                     cl(16, tt), s_qraw[:, 0, tt * 8:(tt + 1) * 8], axis=AX.X),
                     reads=("scr/eq",), writes=(f"cl/v1{tt}",), lazy=True)
            for tt in range(NTT):
                p.op("scalar", lambda e, tt=tt: e.activation(
                     cl(24, tt), cl(16, tt), AF.Exp, bias=cl(8, tt)),
                     reads=(f"cl/v1{tt}", f"cl/nv{tt}"), writes=(f"cl/e1{tt}",))
            for tt in range(NTT):
                p.op("vector", lambda e, tt=tt: e.tensor_scalar_add(
                     cl(32, tt), cl(24, tt), 1.0),
                     reads=(f"cl/e1{tt}",), writes=(f"cl/Z{tt}",))
            for tt in range(NTT):
                p.op("vector", lambda e, tt=tt: e.reciprocal(
                     cl(40, tt), cl(32, tt)),
                     reads=(f"cl/Z{tt}",), writes=(f"cl/w0{tt}",), lazy=True)
            for tt in range(NTT):
                p.op("vector", lambda e, tt=tt: e.tensor_mul(
                     cl(48, tt), cl(24, tt), cl(40, tt)),
                     reads=(f"cl/e1{tt}", f"cl/w0{tt}"), writes=(f"cl/w1{tt}",))
            for tt in range(NTT):
                p.op("vector", lambda e, tt=tt: e.tensor_add(
                     comb[:, tt:tt + 1], cl(40, tt), cl(48, tt)),
                     reads=(f"cl/w0{tt}", f"cl/w1{tt}"), writes=(f"comb/{tt}",))

        def mlp(c, nmega, wg_h, wu_h, dw_h, routed):
            if not routed:
                fk = p.fence("scr/")
                p.seed(fk, "scr/sig0", "scr/sig1")
                fh = p.fence("kfm/", "vtm/")
                p.seed(fh, "wgu/0", "wgu/1",
                       *[f"hbuf/{ic}/{tg}" for ic in range(8) for tg in (0, 1)])
            else:
                fk, fh = {}, {}
            first_acc = not routed
            slg = [0]
            for m in range(nmega):
                for ic in range(8):
                    icg = m * 8 + ic
                    pg, pu = slg[0] % 2, (slg[0] + 1) % 2
                    stream_slab(c, pg, "wgu", wg_h,
                                lambda t, icg=icg: t[icg, :, :, :])
                    stream_slab(c, pu, "wgu", wu_h,
                                lambda t, icg=icg: t[icg, :, :, :])
                    slg[0] += 2
                    for tg in range(2):
                        bg, bu = (0, 1) if tg == 0 else (2, 3)
                        spar = (ic * 2 + tg) % 2
                        rhs_reads = tuple(f"lnfm/{dc}/{tt}" for dc in range(NDC)
                                          for tt in range(tg * 4, tg * 4 + 4))
                        for dc in range(NDC):
                            p.op("tensor", lambda e, dc=dc, pg=pg, bg=bg, tg=tg:
                                 e.matmul(pb[bg][:, :], wgu[:, pg, dc, :],
                                          lnfm[:, dc, tg * 512:(tg + 1) * 512],
                                          start=(dc == 0), stop=(dc == NDC - 1)),
                                 reads=(f"wgu/{pg}",) + rhs_reads,
                                 writes=(f"ps/{bg}",))
                        for dc in range(NDC):
                            p.op("tensor", lambda e, dc=dc, pu=pu, bu=bu, tg=tg:
                                 e.matmul(pb[bu][:, :], wgu[:, pu, dc, :],
                                          lnfm[:, dc, tg * 512:(tg + 1) * 512],
                                          start=(dc == 0), stop=(dc == NDC - 1)),
                                 reads=(f"wgu/{pu}",) + rhs_reads,
                                 writes=(f"ps/{bu}",))
                        p.op("scalar", lambda e, bg=bg, spar=spar: e.activation(
                             s_sig[:, spar, :], pb[bg][:], AF.Sigmoid),
                             reads=(f"ps/{bg}",), writes=(f"scr/sig{spar}",))
                        p.op("vector", lambda e, bg=bg, spar=spar: e.tensor_mul(
                             s_sig[:, spar, :], s_sig[:, spar, :], pb[bg][:]),
                             reads=(f"ps/{bg}", f"scr/sig{spar}"),
                             writes=(f"scr/sig{spar}",))
                        p.op("vector", lambda e, bu=bu, spar=spar, ic=ic, tg=tg:
                             e.tensor_mul(h_buf[:, ic, tg * 512:(tg + 1) * 512],
                                          s_sig[:, spar, :], pb[bu][:]),
                             reads=(f"ps/{bu}", f"scr/sig{spar}"),
                             writes=(f"hbuf/{ic}/{tg}",))
                for ogq in range(4):
                    par = (m * 4 + ogq) % 2
                    if not routed:
                        stream_slab(c, par, "wodw", "dwm",
                                    lambda t, m=m, ogq=ogq: t[m, ogq, :, :, :])
                    else:
                        stream_slab(c, par, "wodw", "rd",
                                    lambda t, ogq=ogq: t[ogq, :, :, :])
                    hreads = tuple(f"hbuf/{ic}/{tg}" for ic in range(8)
                                   for tg in range(2))
                    for tt in range(NTT):
                        bank = 4 + (tt % 2)
                        for ic in range(8):
                            p.op("tensor", lambda e, ic=ic, tt=tt, par=par, bank=bank:
                                 e.matmul(pb[bank][:, :],
                                          h_buf[:, ic, tt * 128:(tt + 1) * 128],
                                          wodw[:, par, ic, :],
                                          start=(ic == 0), stop=(ic == 7)),
                                 reads=(f"wodw/{par}",) + hreads,
                                 writes=(f"ps/{bank}",))
                        dst = xa[:, tt, ogq * 512:(ogq + 1) * 512]
                        if first_acc and m == 0:
                            p.op("vector", lambda e, dst=dst, bank=bank:
                                 e.tensor_copy(dst, pb[bank][:]),
                                 reads=(f"ps/{bank}",), writes=(f"xa/{tt}",))
                        elif not routed:
                            p.op("vector", lambda e, dst=dst, bank=bank:
                                 e.tensor_add(dst, dst, pb[bank][:]),
                                 reads=(f"ps/{bank}", f"xa/{tt}"),
                                 writes=(f"xa/{tt}",))
                        else:
                            p.op("vector", lambda e, dst=dst, bank=bank, tt=tt:
                                 e.scalar_tensor_tensor(
                                     dst, pb[bank][:], comb[:, tt:tt + 1], dst,
                                     ALU.mult, ALU.add),
                                 reads=(f"ps/{bank}", f"comb/{tt}", f"xa/{tt}"),
                                 writes=(f"xa/{tt}",))

        def phase_ffn(c):
            mlp(c, 8, "gw", "uw", None, routed=False)
            mlp(c, 1, "rg", "ru", None, routed=True)

        def phase_finish(c):
            fk = p.fence("scr/")
            p.seed(fk, "scr/big1", "scr/big2")
            def cl(base, tt):
                return cols[:, base + tt:base + tt + 1]
            for tt in range(NTT):
                p.op("vector", lambda e, tt=tt: e.reduce_sum(
                     cl(0, tt), xa[:, tt, :], axis=AX.X),
                     reads=(f"xa/{tt}",), writes=(f"cl/s{tt}",), lazy=True)
            for tt in range(NTT):
                p.op("scalar", lambda e, tt=tt: e.activation(
                     s_lnsq[:], xa[:, tt, :], AF.Square, accum_out=cl(8, tt)),
                     reads=(f"xa/{tt}",), writes=("scr/big1", f"cl/q{tt}"),
                     lazy=True)
            for tt in range(NTT):
                p.op("vector", lambda e, tt=tt: e.tensor_scalar_mul(
                     cl(16, tt), cl(0, tt), 1.0 / D),
                     reads=(f"cl/s{tt}",), writes=(f"cl/m{tt}",))
            for tt in range(NTT):
                p.op("scalar", lambda e, tt=tt: e.activation(
                     cl(24, tt), cl(16, tt), AF.Square),
                     reads=(f"cl/m{tt}",), writes=(f"cl/m2{tt}",))
            for tt in range(NTT):
                p.op("vector", lambda e, tt=tt: e.scalar_tensor_tensor(
                     cl(32, tt), cl(8, tt), 1.0 / D, cl(24, tt),
                     ALU.mult, ALU.subtract),
                     reads=(f"cl/q{tt}", f"cl/m2{tt}"), writes=(f"cl/v{tt}",))
            for tt in range(NTT):
                p.op("scalar", lambda e, tt=tt: e.activation(
                     cl(40, tt), cl(32, tt), AF.Sqrt, bias=cstc[:, 0:1]),
                     reads=(f"cl/v{tt}", "c/all"), writes=(f"cl/sr{tt}",))
            for tt in range(NTT):
                p.op("vector", lambda e, tt=tt: e.reciprocal(
                     cl(48, tt), cl(40, tt)),
                     reads=(f"cl/sr{tt}",), writes=(f"cl/r{tt}",), lazy=True)
            for tt in range(NTT):
                r0 = c * CT + tt * 128
                p.op("sync", lambda e, r0=r0: e.dma_start(
                     out=s_xstg[:], in_=out_h[r0:r0 + 128, :]),
                     reads=(f"out/{c}/{tt}",), writes=("scr/big2",), dma="xg")
                p.op("vector", lambda e, tt=tt: e.tensor_scalar(
                     s_tmp8[:], xa[:, tt, :], cl(16, tt), cl(48, tt),
                     ALU.subtract, ALU.mult),
                     reads=(f"xa/{tt}", f"cl/m{tt}", f"cl/r{tt}"),
                     writes=("scr/big1",))
                p.op("vector", lambda e: e.tensor_add(
                     s_tmp8[:], s_tmp8[:], s_xstg[:]),
                     reads=("scr/big1", "scr/big2"), writes=("scr/big1",))
                p.op("sync", lambda e, r0=r0: e.dma_start(
                     out=out_h[r0:r0 + 128, :], in_=s_tmp8[:]),
                     reads=("scr/big1",), writes=(f"out/{c}/{tt}",), dma="st")

        for c in range(nchunk):
            for tt in range(NTT):
                r0 = c * CT + tt * 128
                p.op("sync", lambda e, r0=r0: e.dma_start(
                     out=xa[:, (r0 // 128) % NTT, :], in_=x_h[r0:r0 + 128, :]),
                     writes=(f"xa/{tt}",), dma=f"x{tt}")
            p.op("sync", lambda e, c=c: e.dma_start(
                 out=cst2[:, :, :], in_=dr["cstab"][:, :, c * CT:(c + 1) * CT]),
                 writes=("c/cos",), dma="cs")
            phase_ln(c, dump=dbg_h.get("lnfm") if c == 0 else None)
            phase_qkv(c)
            if "qfm" in dbg_h and c == 0:
                p.op("sync", lambda e: e.dma_start(out=dbg_h["qfm"], in_=qa[:]),
                     reads=tuple(f"qa/{h}/{qt}" for h in range(NH)
                                 for qt in range(NTT)), dma="dbg")
            if "kfm" in dbg_h and c == 0:
                p.op("sync", lambda e: e.dma_start(out=dbg_h["kfm"], in_=k_fm[:]),
                     reads=tuple(f"kfm/{kv}/{t}" for kv in range(NKV)
                                 for t in range(NTT)), dma="dbg")
            if "vtm" in dbg_h and c == 0:
                p.op("sync", lambda e: e.dma_start(out=dbg_h["vtm"], in_=v_tm[:]),
                     reads=tuple(f"vtm/{t}" for t in range(1 + NTT)), dma="dbg")
            phase_wo_prefetch(c)
            phase_attn(c)
            if "adbg" in dbg_h and c == 0:
                p.op("sync", lambda e: e.dma_start(
                     out=dbg_h["adbg"][:, :, 0:256], in_=s_sc[:, 0:2, :]),
                     reads=("scr/sc0", "scr/sc1", "scr/sc2", "scr/sc3"), dma="dbg")
                p.op("sync", lambda e: e.dma_start(
                     out=dbg_h["adbg"][:, :, 256:512], in_=s_e[:, 0:2, :]),
                     reads=("scr/e0", "scr/e1", "scr/e2", "scr/e3"), dma="dbg")
                p.op("sync", lambda e: e.dma_start(
                     out=dbg_h["adbg"][:, :, 512:544],
                     in_=cols[:, 0:64].rearrange("p (a b) -> p a b", a=2)),
                     reads=tuple(f"cl/a{i}/{j}" for i in range(6)
                                 for j in range(3)), dma="dbg")
                if "adbg2" in dbg_h:
                    p.op("sync", lambda e: e.dma_start(
                         out=dbg_h["adbg2"][:, :, 0:256], in_=s_ebf[:, 0:2, :]),
                         reads=tuple(f"scr/ebf{i}" for i in range(6)), dma="dbg")
                    p.op("sync", lambda e: e.dma_start(
                         out=dbg_h["adbg2"][:, :, 256:512], in_=s_aT[:]),
                         reads=("scr/aT0", "scr/aT1"), dma="dbg")
            if "attno" in dbg_h and c == 0:
                p.op("sync", lambda e: e.dma_start(out=dbg_h["attno"], in_=qa[:]),
                     reads=tuple(f"qa/{h}/{qt}" for h in range(NH)
                                 for qt in range(NTT)), dma="dbg")
            phase_oproj_ln1(c)
            if "xmid" in dbg_h and c == 0:
                p.op("sync", lambda e: e.dma_start(
                     out=dbg_h["xmid"].rearrange("(a p) d -> p a d", p=128),
                     in_=xa[:]),
                     reads=tuple(f"xa/{tt}" for tt in range(NTT)), dma="dbg")
            phase_ln(c)
            phase_spill(c)
            phase_router(c)
            if "comb" in dbg_h and c == 0:
                p.op("sync", lambda e: e.dma_start(out=dbg_h["comb"], in_=comb[:]),
                     reads=tuple(f"comb/{tt}" for tt in range(NTT)), dma="dbg")
            phase_ffn(c)
            if "oacc" in dbg_h and c == 0:
                p.op("sync", lambda e: e.dma_start(out=dbg_h["oacc"], in_=xa[:]),
                     reads=tuple(f"xa/{tt}" for tt in range(NTT)), dma="dbg")
            phase_finish(c)
            if c == 0:
                for mcls in ("mA", "mG", "mD"):
                    mirror_tot[mcls] = p.cnt.get("d_" + mcls, 0)

        global _LAST_PLAN
        _LAST_PLAN = p
        if "mdbg" in dbg_h:
            p.op("sync", lambda e: e.dma_start(
                 out=dbg_h["mdbg"][:, 0:256], in_=mkeep),
                 reads=("c/mkeep",), dma="dbg")
            p.op("sync", lambda e: e.dma_start(
                 out=dbg_h["mdbg"][:, 256:512], in_=mfill),
                 reads=("c/all",), dma="dbg")
        finals = [(p.semh[nm], cval) for nm, cval in p.cnt.items()
                  if nm.startswith("d_") and cval > 0]
        def final_wait(e, finals=finals):
            for h, v in finals:
                e.wait_ge(h, v)
        p.raw("sync", final_wait)

        block = E(nc.Block())
        for eng in ENGS:
            ops = p.q[eng]
            if not ops:
                continue
            getattr(block, eng)(lambda e, ops=ops: [f(e) for f in ops])

    return nc



def kernel(**inputs):
    shared = _host_prep(inputs)
    x = np.ascontiguousarray(np.asarray(inputs["L0_attn_x"], np.float32))
    nc = build(NCHUNK_FULL)
    in_maps = []
    for ci in range(8):
        m = dict(shared)
        m["x"] = np.ascontiguousarray(x[ci])
        in_maps.append(m)
    res = run_bass_kernel_spmd(nc, in_maps, core_ids=list(range(8)))
    return np.stack([r["out"] for r in res.results], axis=0)
